# version 28
# baseline (speedup 1.0000x reference)
import numpy as np
from contextlib import ExitStack

import concourse.bass as bass
import concourse.mybir as mybir
from concourse.alu_op_type import AluOpType
from concourse.bass_utils import run_bass_kernel_spmd

F32 = mybir.dt.float32
BF16 = mybir.dt.bfloat16
U32 = mybir.dt.uint32
AF = mybir.ActivationFunctionType
AX = mybir.AxisListType
IOffs = bass.IndirectOffsetOnAxis

B = 4
SEQ = 2048
D = 2048
DCH = D // 128
NT = 8
TOK = NT * 128
HEADS = 16
GR = 4
HPG = 4
HD = 128
SCALE = float(1.0 / np.sqrt(HD))
NBLK = [1, 1, 2, 2, 3, 3, 4, 4]
TILES_EVEN = [0, 3, 4, 7, 8, 11, 12, 15]
TILES_ODD = [1, 2, 5, 6, 9, 10, 13, 14]

DC = 8192
NCC = DC // 128
DP = 1024
NPC = DP // 128
DR = 1024
NRC = DR // 128
NE = 16384
NH = 8
KP = 8
NSLOT = NH * KP
MASK_NEG = -1.0e30


class Sem:
    def __init__(self, es, nc, name):
        self.h = es.enter_context(nc.semaphore(name))
        self.n = 0

    def wait(self, eng, v=None):
        eng.wait_ge(self.h, self.n if v is None else v)


def build_nc():
    nc = bass.Bass("TRN2", target_bir_lowering=False)

    dx_own = nc.dram_tensor("x_own", [TOK, D], F32, kind="ExternalInput").ap()
    dx_seq = nc.dram_tensor("x_seq", [SEQ, D], F32, kind="ExternalInput").ap()
    dqidx = nc.dram_tensor("qidx", [128, NT], F32, kind="ExternalInput").ap()
    dw_q = nc.dram_tensor("w_q", [HEADS, 128, DCH, 128], F32, kind="ExternalInput").ap()
    dw_kv = nc.dram_tensor("w_kv", [8, 128, DCH, 128], F32, kind="ExternalInput").ap()
    dw_de = nc.dram_tensor("w_dense", [DCH, 128, DCH, 128], F32, kind="ExternalInput").ap()
    dw_up = nc.dram_tensor("w_up", [NCC, 128, DCH, 128], F32, kind="ExternalInput").ap()
    db_up = nc.dram_tensor("b_up", [DC], F32, kind="ExternalInput").ap()
    dw_dn = nc.dram_tensor("w_down", [NPC, 8, 128, 8, 128], F32, kind="ExternalInput").ap()
    db_dn = nc.dram_tensor("b_down", [DP], F32, kind="ExternalInput").ap()
    dw_qr = nc.dram_tensor("w_qr", [DP, DR], F32, kind="ExternalInput").ap()
    dkeys = nc.dram_tensor("keys", [NH, 128, 2, 64], F32, kind="ExternalInput").ap()
    d_tab = nc.dram_tensor("tab", [NE, DP + D], F32, kind="ExternalInput").ap()
    d_tab_bf = nc.dram_tensor("tab_bf", [NE, DP + D], BF16, kind="Internal").ap()
    dout = nc.dram_tensor("out", [TOK, D], F32, kind="ExternalOutput").ap()
    dpark = nc.dram_tensor("r_park", [TOK, D], F32, kind="Internal").ap()

    es0 = ExitStack()

    def sb(esx, name, shape, dt, side=None):
        if side is None:
            return esx.enter_context(nc.sbuf_tensor(name, shape, dt))
        return esx.enter_context(nc.sbuf_tensor(name, shape, dt, side=side))

    def psum(esx, name, shape, dt):
        return esx.enter_context(nc.psum_tensor(name, shape, dt))

    ident_f = sb(es0, "ident_f", [128, 128], F32)
    ident_b = sb(es0, "ident_b", [128, 128], BF16)
    iota_cr = sb(es0, "iota_cr", [128, 128], F32)
    qidxT = sb(es0, "qidxT", [128, NT], F32)
    keysT = sb(es0, "keysT", [128, 2, NH, 128], BF16)
    b_upT = sb(es0, "b_upT", [128, NCC], F32)
    b_dnT = sb(es0, "b_dnT", [128, NPC], F32)
    epsT = sb(es0, "epsT", [128, 1], F32)
    stat6 = sb(es0, "stat6", [128, 4, 6], F32)
    meanvar = sb(es0, "meanvar", [128, 2], F32)
    negmean = sb(es0, "negmean", [128, 1], F32)
    invstd = sb(es0, "invstd", [128, 1], F32)
    nr_t = sb(es0, "nr_t", [128, 1], F32)
    mx1 = sb(es0, "mx1", [128, 1], F32)
    sum1 = sb(es0, "sum1", [128, 2], F32)
    negsc = sb(es0, "negsc", [128, 2], F32)

    E_all = sb(es0, "E_all", [128, NT, NSLOT], U32)
    G_all = sb(es0, "G_all", [128, NT, NSLOT], F32)
    mx01 = sb(es0, "mx01", [128, 2, NSLOT], F32, side="right")
    ix01 = sb(es0, "ix01", [128, 2, NSLOT], U32, side="right")
    allsc = sb(es0, "allsc", [128, NH, 64], F32, side="right")
    sc8 = sb(es0, "sc8", [128, NH, KP], F32, side="right")
    pk8 = sb(es0, "pk8", [128, NH, KP], U32, side="right")
    pkd = sb(es0, "pkd", [128, NH, KP], U32, side="right")
    pkm = sb(es0, "pkm", [128, NH, KP], U32, side="right")
    iotaj = sb(es0, "iotaj", [128, KP], F32)
    tsel = sb(es0, "tsel", [128, NH, KP, KP], F32, side="right")
    exK = sb(es0, "exK", [128, NH, KP], F32, side="right")
    eyK = sb(es0, "eyK", [128, NH, KP], F32, side="right")
    gex = sb(es0, "gex", [128, NH, KP], F32, side="right")
    gsum = sb(es0, "gsum", [128, NH], F32)
    negg = sb(es0, "negg", [128, NH], F32)

    ps_big = psum(es0, "ps_big", [128, 2048], F32)
    ps_a = psum(es0, "ps_a", [128, 512], F32)
    ps_c = psum(es0, "ps_c", [128, 512], F32)
    ps_bf = psum(es0, "ps_bf", [128, 2048], BF16)

    S = lambda name: Sem(es0, nc, name)
    s_ld = S("s_ld")
    s_ldk = S("s_ldk")
    s_ldx = [S(f"s_ldx{i}") for i in range(NT)]
    s_ldb = [S(f"s_ldb{i}") for i in range(4)]
    s_v = S("s_v")
    s_sc = S("s_sc")
    s_pe = S("s_pe")
    s_g = S("s_g")
    s_ev = S("s_ev")
    s_bar = S("s_bar")
    s_park = S("s_park")
    s_out2 = [S(f"s_out{i}") for i in range(2)]
    s_de = [S(f"s_de{i}") for i in range(16)]
    s_ue = [S(f"s_ue{i}") for i in range(4)]
    s_vde = S("s_vde")
    s_vue = S("s_vue")
    s_zg = S("s_zg")
    s_r = [S(f"s_r{i}") for i in range(2)]
    ch_v = S("ch_v")
    ch_g = S("ch_g")

    s_sgin = [S(f"s_sgin{i}") for i in range(2)]
    s_sgout = [S(f"s_sgout{i}") for i in range(2)]
    s_sgc = S("s_sgc")
    esStg = ExitStack()
    stg_f = sb(esStg, "stg_f", [128, 2, 3072], F32, side="right")
    stg_b = sb(esStg, "stg_b", [128, 2, 3072], BF16, side="right")
    stage_state = {"i": 0}
    N_STG = NE // 128

    blk = es0.enter_context(nc.Block())

    def emit_stage_chunk(n=1):
        n = min(n, N_STG - stage_state["i"])
        if n <= 0:
            return
        chunks = []
        for _ in range(n):
            i = stage_state["i"]
            stage_state["i"] = i + 1
            chunks.append((i, i % 2))

        @blk.gpsimd
        def _(g, chunks=chunks):
            for i, buf in chunks:
                if i >= 2:
                    g.wait_ge(s_sgc.h, i - 1)
                g.dma_start(out=stg_f[:, buf, :],
                            in_=d_tab[i * 128:(i + 1) * 128, :]
                            ).then_inc(s_sgin[buf].h, 16)

        @blk.scalar
        def _(sc, chunks=chunks):
            for i, buf in chunks:
                sc.wait_ge(s_sgin[buf].h, 16 * (i // 2 + 1))
                if i >= 2:
                    sc.wait_ge(s_sgout[buf].h, 16 * (i // 2))
                sc.activation(out=stg_b[:, buf, :], in_=stg_f[:, buf, :],
                              func=AF.Copy).then_inc(s_sgc.h, 1)
                sc.wait_ge(s_sgc.h, i + 1)
                sc.dma_start(out=d_tab_bf[i * 128:(i + 1) * 128, :],
                             in_=stg_b[:, buf, :]).then_inc(s_sgout[buf].h, 16)

        for i, buf in chunks:
            s_sgin[buf].n += 16
            s_sgc.n += 1
            s_sgout[buf].n += 16

    def barrier(light=False):
        decs = ((blk.sync, blk.tensor, blk.vector) if light else
                (blk.sync, blk.tensor, blk.vector, blk.scalar, blk.gpsimd))
        for dec in decs:
            @dec
            def _(e):
                e.drain()
                e.sem_inc(s_bar.h, 1)
        s_bar.n += len(decs)
        for dec in decs:
            @dec
            def _(e):
                s_bar.wait(e)

    esInit = ExitStack()
    bload = sb(esInit, "bload", [64, 128], F32)
    bload2 = sb(esInit, "bload2", [8, 128], F32)

    @blk.sync
    def _(sy):
        sy.dma_start(out=qidxT[:], in_=dqidx[:]).then_inc(s_ld.h, 16)
        sy.dma_start(out=bload[:],
                     in_=db_up[:].rearrange("(a p) -> a p", a=64)
                     ).then_inc(s_ld.h, 16)
        sy.dma_start(out=bload2[:],
                     in_=db_dn[:].rearrange("(a p) -> a p", a=8)
                     ).then_inc(s_ld.h, 16)
    s_ld.n += 48

    @blk.tensor
    def _(t):
        s_v.wait(t, 1)
        s_ld.wait(t, 48)
        t.transpose(ps_big[:, 0:64], bload[:], ident_f[0:64, 0:64]
                    ).then_inc(s_pe.h, 1)
        t.transpose(ps_big[:, 512:512 + 8], bload2[:], ident_f[0:8, 0:8]
                    ).then_inc(s_pe.h, 1)
    s_pe.n += 2

    @blk.scalar
    def _(sc):
        s_pe.wait(sc)
        sc.activation(out=b_upT[:], in_=ps_big[:, 0:64], func=AF.Copy
                      ).then_inc(s_sc.h, 1)
        sc.activation(out=b_dnT[:], in_=ps_big[:, 512:512 + 8], func=AF.Copy
                      ).then_inc(s_sc.h, 1)
    s_sc.n += 2

    @blk.gpsimd
    def _(g):
        g.iota(iota_cr[:], pattern=[[1, 128]], base=0, channel_multiplier=-1,
               allow_small_or_imprecise_dtypes=True)
        g.iota(iotaj[:], pattern=[[1, KP]], base=0, channel_multiplier=0,
               allow_small_or_imprecise_dtypes=True).then_inc(s_g.h, 1)
    s_g.n += 1

    @blk.vector
    def _(v):
        v.memset(epsT[:], 1e-5)
        s_g.wait(v)
        v.tensor_scalar(out=ident_f[:], in0=iota_cr[:], scalar1=0.0, scalar2=None,
                        op0=AluOpType.is_equal).then_inc(ch_v.h, 1)
        ch_v.n += 1
        ch_v.wait(v)
        v.tensor_copy(out=ident_b[:], in_=ident_f[:]).then_inc(s_v.h, 1)
    s_v.n += 1

    kbuf = sb(esInit, "kbuf", [128, 64], F32)
    kbufb = sb(esInit, "kbufb", [128, 64], BF16)
    for h in range(NH):
        for p in range(2):
            @blk.sync
            def _(sy, h=h, p=p):
                s_sc.wait(sy)
                sy.dma_start(out=kbuf[:], in_=dkeys[h, :, p, :]).then_inc(s_ldk.h, 16)
            s_ldk.n += 16

            @blk.vector
            def _(v):
                s_ldk.wait(v)
                s_pe.wait(v)
                v.tensor_copy(out=kbufb[:], in_=kbuf[:]).then_inc(s_v.h, 1)
            s_v.n += 1

            @blk.tensor
            def _(t):
                s_v.wait(t)
                t.transpose(ps_bf[0:64, 0:128], kbufb[:], ident_b[:]
                            ).then_inc(s_pe.h, 1)
            s_pe.n += 1

            @blk.scalar
            def _(sc, h=h, p=p):
                s_pe.wait(sc)
                sc.activation(out=keysT[0:64, p, h, :], in_=ps_bf[0:64, 0:128],
                              func=AF.Copy).then_inc(s_sc.h, 1)
                sc.activation(out=keysT[64:128, p, h, :], in_=ps_bf[0:64, 0:128],
                              func=AF.Copy).then_inc(s_sc.h, 1)
            s_sc.n += 2

    def emit_ln(src_ap, dst_ap):
        @blk.vector
        def _(v):
            for i in range(4):
                v.bn_stats(out=stat6[:, i, :],
                           in_=src_ap[:, 512 * i:512 * (i + 1)]
                           ).then_inc(ch_v.h, 1)
                ch_v.n += 1
            ch_v.wait(v)
            v.bn_aggr(out=meanvar[:], in_=stat6[:].rearrange("p a b -> p (a b)")
                      ).then_inc(ch_v.h, 1)
            ch_v.n += 1
            ch_v.wait(v)
            v.tensor_scalar(out=negmean[:], in0=meanvar[:, 0:1], scalar1=-1.0,
                            scalar2=None, op0=AluOpType.mult).then_inc(s_v.h, 1)
        s_v.n += 1

        @blk.scalar
        def _(sc):
            s_v.wait(sc)
            sc.activation(out=invstd[:], in_=meanvar[:, 1:2], func=AF.Sqrt,
                          bias=epsT[:]).then_inc(s_sc.h, 1)
        s_sc.n += 1

        @blk.vector
        def _(v):
            s_sc.wait(v)
            s_pe.wait(v)
            v.reciprocal(out=invstd[:], in_=invstd[:]).then_inc(ch_v.h, 1)
            ch_v.n += 1
            ch_v.wait(v)
            v.tensor_tensor(out=nr_t[:], in0=invstd[:], in1=invstd[:],
                            op=AluOpType.mult).then_inc(ch_v.h, 1)
            ch_v.n += 1
            ch_v.wait(v)
            v.tensor_scalar(out=nr_t[:], in0=meanvar[:, 1:2], scalar1=nr_t[:],
                            scalar2=-0.5, op0=AluOpType.mult, op1=AluOpType.mult
                            ).then_inc(ch_v.h, 1)
            ch_v.n += 1
            ch_v.wait(v)
            v.tensor_scalar(out=nr_t[:], in0=nr_t[:], scalar1=1.5, scalar2=None,
                            op0=AluOpType.add).then_inc(ch_v.h, 1)
            ch_v.n += 1
            ch_v.wait(v)
            v.tensor_tensor(out=invstd[:], in0=invstd[:], in1=nr_t[:],
                            op=AluOpType.mult).then_inc(ch_v.h, 1)
            ch_v.n += 1
            ch_v.wait(v)
            v.tensor_scalar(out=dst_ap, in0=src_ap, scalar1=negmean[:],
                            scalar2=invstd[:], op0=AluOpType.add, op1=AluOpType.mult
                            ).then_inc(s_v.h, 1)
        s_v.n += 1

    def emit_transpose16(src_fn, dst_fn, n_chunks=16, ident=None):
        ident = ident_b[:] if ident is None else ident
        for g4 in range(0, n_chunks, 4):
            cnt = min(4, n_chunks - g4)

            @blk.tensor
            def _(t, g4=g4, cnt=cnt):
                s_ev.wait(t)
                s_v.wait(t)
                for j in range(cnt):
                    ins = t.transpose(ps_bf[:, j * 256:j * 256 + 128],
                                      src_fn(g4 + j), ident)
                    if j == cnt - 1:
                        ins.then_inc(s_pe.h, 1)
            s_pe.n += 1

            @blk.vector
            def _(v, g4=g4, cnt=cnt):
                s_pe.wait(v)
                for j in range(cnt):
                    v.tensor_copy(out=dst_fn(g4 + j),
                                  in_=ps_bf[:, j * 256:j * 256 + 128]
                                  ).then_inc(s_ev.h, 1)
            s_ev.n += cnt

    barrier()
    esInit.close()
    esA = ExitStack()
    esR = ExitStack()
    R = sb(esR, "R", [128, NT, D], F32, side="right")
    xn_bf = sb(esA, "xn_bf", [128, D], BF16)
    esQT = ExitStack()
    q_T = sb(esQT, "q_T", [128, HEADS, TOK], BF16)
    esQ = ExitStack()
    xnT_own = sb(esQ, "xnT_own", [128, DCH, TOK], BF16)

    for t in range(NT):
        @blk.sync
        def _(sy, t=t):
            sy.dma_start(out=R[:, t, :], in_=dx_own[t * 128:(t + 1) * 128, :]
                         ).then_inc(s_ldx[t].h, 16)
        s_ldx[t].n += 16

        @blk.vector
        def _(v, t=t):
            v.wait_ge(s_ldx[t].h, 16)
        emit_ln(R[:, t, :], xn_bf[:])
        emit_transpose16(lambda c: xn_bf[:, c * 128:(c + 1) * 128],
                         lambda c, t=t: xnT_own[:, c, t * 128:(t + 1) * 128])
        emit_stage_chunk(1)

    wqe = sb(esQ, "wqe", [128, 1, DCH, 128], BF16)
    wqe_f = sb(esQ, "wqe_f", [128, DCH, 128], F32)

    def emit_proj_stream(w_dram, n_et, rhs_fn, out_fn):
        for et in range(n_et):
            buf = et % 2

            @blk.sync
            def _(sy, et=et, buf=buf):
                s_v.wait(sy, s_v.n)
                if et >= 2:
                    s_pe.wait(sy, s_pe.n)
                sy.dma_start(out=wqe_f[:], in_=w_dram[et]
                             ).then_inc(s_ldb[buf].h, 16)
            s_ldb[buf].n += 16

            @blk.vector
            def _(v, buf=buf):
                s_ldb[buf].wait(v)
                s_pe.wait(v, s_pe.n)
                v.tensor_copy(out=wqe[:, 0, :, :], in_=wqe_f[:, :, :]).then_inc(s_v.h, 1)
            s_v.n += 1

            for nt in range(2):
                pst = ps_a if nt == 0 else ps_c

                @blk.tensor
                def _(t, et=et, buf=buf, nt=nt, pst=pst):
                    s_v.wait(t)
                    s_ev.wait(t)
                    for dt in range(DCH):
                        ins = t.matmul(pst[:], wqe[:, 0, dt, :], rhs_fn(dt, nt),
                                       start=(dt == 0), stop=(dt == DCH - 1))
                    ins.then_inc(s_pe.h, 1)
                s_pe.n += 1

                @blk.vector
                def _(v, et=et, nt=nt, pst=pst):
                    s_pe.wait(v)
                    v.tensor_scalar(out=out_fn(et, nt), in0=pst[:], scalar1=SCALE,
                                    scalar2=None, op0=AluOpType.mult
                                    ).then_inc(s_ev.h, 1)
                s_ev.n += 1

    emit_proj_stream(dw_q, HEADS,
                     lambda dt, nt: xnT_own[:, dt, nt * 512:(nt + 1) * 512],
                     lambda et, nt: q_T[:, et, nt * 512:(nt + 1) * 512])

    barrier(light=True)
    esQ.close()

    @blk.sync
    def _(sy):
        s_v.wait(sy)
        s_ev.wait(sy)
        for t in range(NT):
            sy.dma_start(out=dpark[t * 128:(t + 1) * 128, :], in_=R[:, t, :]
                         ).then_inc(s_park.h, 16)
        sy.wait_ge(s_park.h, s_park.n + 16 * NT)
    s_park.n += 16 * NT
    barrier(light=True)
    esR.close()

    kT = sb(esQT, "kT", [128, GR, SEQ], BF16)
    v_km = sb(esQT, "v_km", [128, GR, SEQ], BF16)

    esA2 = ExitStack()
    xsq = sb(esA2, "xsq", [128, D], F32)
    xn_blk = sb(esA2, "xn_blk", [128, DCH, 512], BF16)
    vT = sb(esA2, "vT", [128, GR, 512], BF16)
    wkv_f = sb(esA2, "wkv_f", [128, DCH, 128], F32)
    wkv_b = sb(esA2, "wkv_b", [128, 2, DCH, 128], BF16)
    kv_cast = {0: 0, 1: 0}
    kv_mm = {0: 0, 1: 0}

    for bidx in range(SEQ // 512):
        for tt in range(4):
            @blk.sync
            def _(sy, bidx=bidx, tt=tt):
                s_v.wait(sy, s_v.n)
                sy.dma_start(out=xsq[:],
                             in_=dx_seq[(bidx * 4 + tt) * 128:
                                        (bidx * 4 + tt + 1) * 128, :]
                             ).then_inc(s_ldb[2].h, 16)
            s_ldb[2].n += 16

            @blk.vector
            def _(v):
                s_ldb[2].wait(v)
            emit_ln(xsq[:], xn_bf[:])
            emit_transpose16(lambda c: xn_bf[:, c * 128:(c + 1) * 128],
                             lambda c, tt=tt:
                             xn_blk[:, c, tt * 128:(tt + 1) * 128])
            emit_stage_chunk(1)

        for et in range(8):
            buf = et % 2
            pst = ps_a if et % 2 == 0 else ps_c
            dst = kT if et < 4 else vT
            g = et % 4
            dcol = bidx * 512 if et < 4 else 0

            @blk.sync
            def _(sy, et=et, buf=buf):
                if kv_cast[0] or kv_cast[1]:
                    sy.wait_ge(s_v.h, max(kv_cast[0], kv_cast[1]))
                sy.dma_start(out=wkv_f[:], in_=dw_kv[et]
                             ).then_inc(s_ldb[3].h, 16)
            s_ldb[3].n += 16

            @blk.vector
            def _(v, et=et, buf=buf):
                s_ldb[3].wait(v)
                if kv_mm[buf]:
                    v.wait_ge(s_pe.h, kv_mm[buf])
                v.tensor_copy(out=wkv_b[:, buf, :, :], in_=wkv_f[:, :, :]
                              ).then_inc(s_v.h, 1)
            s_v.n += 1
            kv_cast[buf] = s_v.n

            @blk.tensor
            def _(t, et=et, buf=buf, pst=pst):
                t.wait_ge(s_v.h, kv_cast[buf])
                s_ev.wait(t)
                for dt in range(DCH):
                    ins = t.matmul(pst[:], wkv_b[:, buf, dt, :],
                                   xn_blk[:, dt, :],
                                   start=(dt == 0), stop=(dt == DCH - 1))
                ins.then_inc(s_pe.h, 1)
            s_pe.n += 1
            kv_mm[buf] = s_pe.n

            @blk.vector
            def _(v, bidx=bidx, g=g, dst=dst, pst=pst):
                s_pe.wait(v)
                v.tensor_copy(out=dst[:, g, dcol:dcol + 512], in_=pst[:]
                              ).then_inc(s_ev.h, 1)
            s_ev.n += 1

        emit_transpose16(
            lambda i, bidx=bidx: vT[:, i // 4, (i % 4) * 128:(i % 4) * 128 + 128],
            lambda i, bidx=bidx: v_km[:, i // 4,
                                      bidx * 512 + (i % 4) * 128:
                                      bidx * 512 + (i % 4) * 128 + 128],
            n_chunks=16)

    barrier(light=True)
    esA2.close()

    esB = ExitStack()
    ctx = sb(esB, "ctx", [128, HEADS, TOK], BF16, side="right")
    esB1 = ExitStack()
    maskb_all = sb(esB1, "maskb_all", [128, NT, 512], BF16)
    mios_t = sb(esB1, "mios", [128, 512], F32)
    mios = mios_t[:]
    P = sb(esB1, "P", [128, 2, 2048], BF16)
    P_T = sb(esB1, "P_T", [128, 16, 128], BF16)

    for slot in range(NT):
        @blk.gpsimd
        def _(g, slot=slot):
            if slot > 0:
                s_v.wait(g, s_v.n)
            else:
                g.wait_ge(s_pe.h, s_pe.n)
                g.wait_ge(s_v.h, s_v.n)
            g.iota(mios, pattern=[[1, 512]], base=(NBLK[slot] - 1) * 512,
                   channel_multiplier=0, allow_small_or_imprecise_dtypes=True
                   ).then_inc(s_g.h, 1)
        s_g.n += 1

        @blk.vector
        def _(v, slot=slot):
            s_g.wait(v)
            v.tensor_scalar(out=mios, in0=mios,
                            scalar1=qidxT[:, slot:slot + 1],
                            scalar2=0.0, op0=AluOpType.subtract, op1=AluOpType.max
                            ).then_inc(ch_v.h, 1)
            ch_v.n += 1
            ch_v.wait(v)
            v.tensor_scalar(out=maskb_all[:, slot, :], in0=mios, scalar1=MASK_NEG,
                            scalar2=None, op0=AluOpType.mult).then_inc(s_v.h, 1)
        s_v.n += 1

    pairs = [(s_, h_) for s_ in range(NT) for h_ in range(HEADS)]
    exp_done = {}
    trans_done = {}
    scale_done = {}
    evac_done = {}
    ctxev_done = {}
    half_state = {"g": 0}

    def stage_a(i):
        slot, h = pairs[i]
        nb = NBLK[slot]
        nk = nb * 512
        grp = h // HPG
        par = i % 2

        @blk.tensor
        def _(t, slot=slot, h=h, grp=grp, nb=nb, i=i):
            if i >= 1:
                t.wait_ge(s_sc.h, exp_done[i - 1])
            s_v.wait(t)
            for b in range(nb):
                t.matmul(ps_big[:, b * 512:(b + 1) * 512],
                         q_T[:, h, slot * 128:(slot + 1) * 128],
                         kT[:, grp, b * 512:(b + 1) * 512],
                         start=True, stop=(b < nb - 1))
            ins = t.matmul(ps_big[:, (nb - 1) * 512:nb * 512],
                           ident_b[:], maskb_all[:, slot, :],
                           start=False, stop=True)
            ins.then_inc(s_pe.h, 1)
        s_pe.n += 1

        @blk.vector
        def _(v, nk=nk, par=par):
            s_pe.wait(v)
            v.reduce_max(out=negsc[:, par:par + 1], in_=ps_big[:, 0:nk], axis=AX.X,
                         negate=True).then_inc(s_v.h, 1)
        s_v.n += 1

        @blk.scalar
        def _(sc, nk=nk, par=par, i=i):
            s_v.wait(sc)
            if i >= 2:
                sc.wait_ge(s_pe.h, trans_done[i - 2])
            sc.activation(out=P[:, par, 0:nk], in_=ps_big[:, 0:nk], func=AF.Exp,
                          bias=negsc[:, par:par + 1]).then_inc(s_sc.h, 1)
        s_sc.n += 1
        exp_done[i] = s_sc.n

    def stage_b1(i):
        nk = NBLK[pairs[i][0]] * 512
        par = i % 2

        @blk.vector
        def _(v, nk=nk, par=par, i=i):
            v.wait_ge(s_sc.h, exp_done[i])
            v.reduce_sum(out=sum1[:, par:par + 1], in_=P[:, par, 0:nk], axis=AX.X
                         ).then_inc(ch_v.h, 1)
            ch_v.n += 1
            ch_v.wait(v)
            v.reciprocal(out=sum1[:, par:par + 1], in_=sum1[:, par:par + 1]
                         ).then_inc(ch_v.h, 1)
            ch_v.n += 1
            ch_v.wait(v)
            v.tensor_scalar(out=P[:, par, 0:nk], in0=P[:, par, 0:nk],
                            scalar1=sum1[:, par:par + 1],
                            scalar2=None, op0=AluOpType.mult).then_inc(s_v.h, 1)
        s_v.n += 1
        scale_done[i] = s_v.n

    def stage_b2(i):
        slot, h = pairs[i]
        nb = NBLK[slot]
        nkt = nb * 4
        grp = h // HPG
        par = i % 2
        ops = ps_c if par == 0 else ps_a

        for g4 in range(0, nkt, 4):
            gh = half_state["g"] % 2
            half_state["g"] += 1
            off = gh * 1024

            @blk.tensor
            def _(t, g4=g4, par=par, i=i, off=off, gh=gh):
                t.wait_ge(s_v.h, scale_done[i])
                if gh in evac_done:
                    t.wait_ge(s_ev.h, evac_done[gh])
                for j in range(4):
                    ins = t.transpose(ps_bf[:, off + j * 256:off + j * 256 + 128],
                                      P[:, par, (g4 + j) * 128:(g4 + j + 1) * 128],
                                      ident_b[:])
                ins.then_inc(s_pe.h, 1)
            s_pe.n += 1

            @blk.vector
            def _(v, g4=g4, off=off):
                s_pe.wait(v)
                for j in range(4):
                    v.tensor_copy(out=P_T[:, g4 + j, :],
                                  in_=ps_bf[:, off + j * 256:off + j * 256 + 128]
                                  ).then_inc(s_ev.h, 1)
            s_ev.n += 4
            evac_done[gh] = s_ev.n

            @blk.tensor
            def _(t, g4=g4, grp=grp, i=i, nkt=nkt, ops=ops):
                s_ev.wait(t)
                if g4 == 0 and i >= 2:
                    t.wait_ge(s_ev.h, ctxev_done[i - 2])
                for j in range(4):
                    kt = g4 + j
                    ins = t.matmul(ops[:, 0:128],
                                   v_km[:, grp, kt * 128:(kt + 1) * 128],
                                   P_T[:, kt, :],
                                   start=(kt == 0), stop=(kt == nkt - 1))
                ins.then_inc(s_pe.h, 1)
            s_pe.n += 1
        trans_done[i] = s_pe.n

        @blk.vector
        def _(v, slot=slot, h=h, ops=ops):
            s_pe.wait(v)
            v.tensor_copy(out=ctx[:, h, slot * 128:(slot + 1) * 128],
                          in_=ops[:, 0:128]).then_inc(s_ev.h, 1)
        s_ev.n += 1
        ctxev_done[i] = s_ev.n

    for i in range(len(pairs)):
        if i >= 1:
            stage_b1(i - 1)
        stage_a(i)
        if i >= 1:
            stage_b2(i - 1)
    stage_b1(len(pairs) - 1)
    stage_b2(len(pairs) - 1)

    barrier()
    esB1.close()
    esQT.close()

    wde = sb(esB, "wde", [128, 2, DCH, 128], BF16)
    wde_f = sb(esB, "wde_f", [128, DCH, 128], F32)
    y_f = sb(esB, "y_f", [128, 1024], F32)
    R2 = sb(esB, "R2", [128, NT, D], F32)

    @blk.sync
    def _(sy):
        for t in range(NT):
            sy.dma_start(out=R2[:, t, :], in_=dpark[t * 128:(t + 1) * 128, :]
                         ).then_inc(s_park.h, 16)
    s_park.n += 16 * NT
    _rld_done = s_park.n

    for ht in range(DCH):
        buf = ht % 2

        @blk.sync
        def _(sy, ht=ht, buf=buf):
            s_v.wait(sy, s_v.n)
            if ht >= 2:
                s_pe.wait(sy, s_pe.n)
            sy.dma_start(out=wde_f[:], in_=dw_de[ht]
                         ).then_inc(s_ldb[buf].h, 16)
        s_ldb[buf].n += 16

        @blk.vector
        def _(v, buf=buf):
            s_ldb[buf].wait(v)
            s_pe.wait(v, s_pe.n)
            v.tensor_copy(out=wde[:, buf, :, :], in_=wde_f[:, :, :]).then_inc(s_v.h, 1)
        s_v.n += 1

        for nt in range(2):
            pst = ps_a if nt == 0 else ps_c

            @blk.tensor
            def _(t, buf=buf, nt=nt, pst=pst):
                s_v.wait(t)
                s_ev.wait(t)
                for et in range(DCH):
                    ins = t.matmul(pst[:], wde[:, buf, et, :],
                                   ctx[:, et, nt * 512:(nt + 1) * 512],
                                   start=(et == 0), stop=(et == DCH - 1))
                ins.then_inc(s_pe.h, 1)
            s_pe.n += 1

            @blk.vector
            def _(v, nt=nt, pst=pst):
                s_pe.wait(v)
                v.tensor_copy(out=y_f[:, nt * 512:(nt + 1) * 512], in_=pst[:]
                              ).then_inc(s_ev.h, 1)
            s_ev.n += 1

        for g4 in range(0, 8, 4):
            @blk.tensor
            def _(t, g4=g4):
                s_ev.wait(t)
                s_v.wait(t)
                for j in range(4):
                    ins = t.transpose(ps_big[:, j * 512:j * 512 + 128],
                                      y_f[:, (g4 + j) * 128:(g4 + j + 1) * 128],
                                      ident_f[:])
                ins.then_inc(s_pe.h, 1)
            s_pe.n += 1

            @blk.vector
            def _(v, ht=ht, g4=g4):
                s_pe.wait(v)
                v.wait_ge(s_park.h, _rld_done)
                v.tensor_tensor(
                    out=R2[:, g4:g4 + 4, ht * 128:(ht + 1) * 128],
                    in0=R2[:, g4:g4 + 4, ht * 128:(ht + 1) * 128],
                    in1=ps_big[:].rearrange("p (a b) -> p a b", b=512)[:, :, 0:128],
                    op=AluOpType.add).then_inc(s_v.h, 1)
            s_v.n += 1
        emit_stage_chunk(6)

    barrier()
    esB.close()
    emit_stage_chunk(N_STG - stage_state["i"])

    @blk.sync
    def _(sy):
        sy.wait_ge(s_sgout[0].h, s_sgout[0].n)
        sy.wait_ge(s_sgout[1].h, s_sgout[1].n)
    barrier()
    esStg.close()

    esH = ExitStack()
    h2tok = sb(esH, "h2tok", [128, NT, DP], BF16, side="right")
    qret = sb(esH, "qret", [128, NRC, TOK], BF16, side="right")
    esE = ExitStack()
    xn2T = sb(esE, "xn2T", [128, DCH, TOK], BF16, side="right")
    xn_bf2 = sb(esE, "xn_bf2", [128, D], BF16, side="right")

    for t in range(NT):
        emit_ln(R2[:, t, :], xn_bf2[:])
        emit_transpose16(lambda c: xn_bf2[:, c * 128:(c + 1) * 128],
                         lambda c, t=t: xn2T[:, c, t * 128:(t + 1) * 128])

    @blk.sync
    def _(sy):
        s_v.wait(sy)
        s_ev.wait(sy)
        for t in range(NT):
            sy.dma_start(out=dpark[t * 128:(t + 1) * 128, :], in_=R2[:, t, :]
                         ).then_inc(s_park.h, 16)
        sy.wait_ge(s_park.h, s_park.n + 16 * NT)
    s_park.n += 16 * NT

    barrier()
    esA.close()

    h1g = sb(esE, "h1g", [128, 8, TOK], BF16)
    h2acc = sb(esE, "h2acc", [128, NPC, TOK], F32)
    h2fm = sb(esE, "h2fm", [128, NPC, TOK], BF16)
    wuf = sb(esE, "wuf", [128, 2, DCH, 128], F32)
    wub = sb(esE, "wub", [128, 2, DCH, 128], BF16)
    wqrb = sb(esE, "wqrb", [128, NPC, DR], BF16)

    e_cast = {0: 0, 1: 0}
    e_mm = {0: 0, 1: 0}
    e_psum = {0: 0, 1: 0}
    e_ld = {0: 0, 1: 0}

    for pt in range(NPC):
        buf = pt % 2

        @blk.sync
        def _(sy, pt=pt, buf=buf):
            if e_cast[buf]:
                sy.wait_ge(s_v.h, e_cast[buf])
            sy.dma_start(out=wuf[:, buf, 0:8, :],
                         in_=dw_qr[pt * 128:(pt + 1) * 128, :]
                         ).then_inc(s_ldb[buf].h, 16)
        s_ldb[buf].n += 16
        e_ld[buf] = s_ldb[buf].n

        @blk.vector
        def _(v, pt=pt, buf=buf):
            v.wait_ge(s_ldb[buf].h, e_ld[buf])
            v.tensor_copy(out=wqrb[:, pt, :],
                          in_=wuf[:, buf, 0:8, :]).then_inc(s_v.h, 1)
        s_v.n += 1
        e_cast[buf] = s_v.n

    e_pp = {0: None, 1: None}
    eit = [NPC]

    def pp_wait(t, pr):
        if e_pp[pr] is not None:
            t.wait_ge(e_pp[pr][0], e_pp[pr][1])

    def e_load(src_ap, nch):
        buf = eit[0] % 2
        eit[0] += 1

        @blk.sync
        def _(sy, src_ap=src_ap, nch=nch, buf=buf):
            if e_cast[buf]:
                sy.wait_ge(s_v.h, e_cast[buf])
            sy.dma_start(out=wuf[:, buf, 0:nch, :], in_=src_ap
                         ).then_inc(s_ldb[buf].h, 16)
        s_ldb[buf].n += 16
        e_ld[buf] = s_ldb[buf].n

        @blk.vector
        def _(v, nch=nch, buf=buf):
            v.wait_ge(s_ldb[buf].h, e_ld[buf])
            if e_mm[buf]:
                v.wait_ge(s_pe.h, e_mm[buf])
            v.tensor_copy(out=wub[:, buf, 0:nch, :], in_=wuf[:, buf, 0:nch, :]
                          ).then_inc(s_v.h, 1)
        s_v.n += 1
        e_cast[buf] = s_v.n
        return buf

    for grp in range(8):
        for c8 in range(8):
            cc = grp * 8 + c8
            buf = e_load(dw_up[cc], DCH)
            pr = c8 % 2

            @blk.tensor
            def _(t, buf=buf, pr=pr):
                t.wait_ge(s_v.h, e_cast[buf])
                pp_wait(t, pr)
                for hf in range(2):
                    for dt in range(DCH):
                        ins = t.matmul(
                            ps_big[:, pr * 1024 + hf * 512:
                                   pr * 1024 + hf * 512 + 512],
                            wub[:, buf, dt, :],
                            xn2T[:, dt, hf * 512:hf * 512 + 512],
                            start=(dt == 0), stop=(dt == DCH - 1))
                ins.then_inc(s_pe.h, 1)
            s_pe.n += 1
            e_mm[buf] = s_pe.n

            @blk.scalar
            def _(sc, cc=cc, c8=c8, pr=pr):
                s_pe.wait(sc)
                sc.activation(out=h1g[:, c8, :],
                              in_=ps_big[:, pr * 1024:(pr + 1) * 1024],
                              func=AF.Silu, bias=b_upT[:, cc:cc + 1]
                              ).then_inc(s_sc.h, 1)
            s_sc.n += 1
            e_pp[pr] = (s_sc.h, s_sc.n)
        grp_h1 = s_sc.n

        for pc in range(NPC):
            buf = e_load(dw_dn[pc, grp], 8)
            pr = pc % 2

            @blk.tensor
            def _(t, buf=buf, pr=pr, grp_h1=grp_h1):
                t.wait_ge(s_v.h, e_cast[buf])
                t.wait_ge(s_sc.h, grp_h1)
                pp_wait(t, pr)
                for hf in range(2):
                    for c8 in range(8):
                        ins = t.matmul(
                            ps_big[:, pr * 1024 + hf * 512:
                                   pr * 1024 + hf * 512 + 512],
                            wub[:, buf, c8, :],
                            h1g[:, c8, hf * 512:hf * 512 + 512],
                            start=(c8 == 0), stop=(c8 == 7))
                ins.then_inc(s_pe.h, 1)
            s_pe.n += 1
            e_mm[buf] = s_pe.n

            @blk.vector
            def _(v, pc=pc, grp=grp, pr=pr):
                s_pe.wait(v)
                if grp == 0:
                    v.tensor_copy(out=h2acc[:, pc, :],
                                  in_=ps_big[:, pr * 1024:(pr + 1) * 1024]
                                  ).then_inc(s_ev.h, 1)
                else:
                    v.tensor_tensor(out=h2acc[:, pc, :], in0=h2acc[:, pc, :],
                                    in1=ps_big[:, pr * 1024:(pr + 1) * 1024],
                                    op=AluOpType.add).then_inc(s_ev.h, 1)
            s_ev.n += 1
            e_pp[pr] = (s_ev.h, s_ev.n)

    h2aend = s_ev.n
    for pc in range(NPC):
        @blk.scalar
        def _(sc, pc=pc):
            sc.wait_ge(s_ev.h, h2aend)
            sc.activation(out=h2fm[:, pc, :], in_=h2acc[:, pc, :],
                          func=AF.Identity, bias=b_dnT[:, pc:pc + 1]
                          ).then_inc(s_sc.h, 1)
        s_sc.n += 1
    h2f_done = s_sc.n

    for rt in range(NRC):
        pr = rt % 2

        @blk.tensor
        def _(t, rt=rt, pr=pr):
            t.wait_ge(s_sc.h, h2f_done)
            pp_wait(t, pr)
            for hf in range(2):
                for pcc in range(NPC):
                    ins = t.matmul(
                        ps_big[:, pr * 1024 + hf * 512:
                               pr * 1024 + hf * 512 + 512],
                        wqrb[:, pcc, rt * 128:(rt + 1) * 128],
                        h2fm[:, pcc, hf * 512:hf * 512 + 512],
                        start=(pcc == 0), stop=(pcc == NPC - 1))
            ins.then_inc(s_pe.h, 1)
        s_pe.n += 1

        @blk.vector
        def _(v, rt=rt, pr=pr):
            s_pe.wait(v)
            v.tensor_copy(out=qret[:, rt, :],
                          in_=ps_big[:, pr * 1024:(pr + 1) * 1024]
                          ).then_inc(s_ev.h, 1)
        s_ev.n += 1
        e_pp[pr] = (s_ev.h, s_ev.n)

    barrier()

    for t in range(NT):
        emit_transpose16(lambda c, t=t: h2fm[:, c, t * 128:(t + 1) * 128],
                         lambda c, t=t: h2tok[:, t, c * 128:(c + 1) * 128],
                         n_chunks=NPC)

    barrier()

    topk_done = {}

    def emit_topk(t):
        for ph in range(16):
            p_half, h = divmod(ph, NH)
            chunk = (p_half * 512 + h * 64) // 128
            off = (h % 2) * 64
            sps = ps_a if ph % 2 == 0 else ps_c
            bankoff = (ph % 4 // 2) * 128

            @blk.tensor
            def _(t_, t=t, ph=ph, p_half=p_half, h=h, chunk=chunk, off=off,
                  bankoff=bankoff):
                s_v.wait(t_, s_v.n)
                t_.matmul(sps[:, bankoff:bankoff + 128],
                          qret[off:off + 64, chunk, t * 128:(t + 1) * 128],
                          keysT[off:off + 64, p_half, h, :],
                          start=True, stop=True).then_inc(s_pe.h, 1)
            s_pe.n += 1

            @blk.vector
            def _(v, p_half=p_half, h=h, sps=sps, bankoff=bankoff):
                s_pe.wait(v)
                v.max(out=mx01[:, p_half, h * KP:(h + 1) * KP],
                      in_=sps[:, bankoff:bankoff + 128]).then_inc(ch_v.h, 1)
                ch_v.n += 1
                ch_v.wait(v)
                v.max_index(out=mx01[:, p_half, h * KP:(h + 1) * KP][:, 0:KP] if False else ix01[:, p_half, h * KP:(h + 1) * KP],
                            in_max=mx01[:, p_half, h * KP:(h + 1) * KP],
                            in_values=sps[:, bankoff:bankoff + 128]
                            ).then_inc(s_v.h, 1)
            s_v.n += 1

        @blk.vector
        def _(v, t=t):
            v.tensor_tensor(
                out=allsc[:].rearrange("p h (i j) -> p h i j", i=KP),
                in0=mx01[:, 0, :].rearrange("p (h i) -> p h i", i=KP)
                    [:, :, :, None].to_broadcast([128, NH, KP, KP]),
                in1=mx01[:, 1, :].rearrange("p (h j) -> p h j", j=KP)
                    [:, :, None, :].to_broadcast([128, NH, KP, KP]),
                op=AluOpType.add).then_inc(ch_v.h, 1)
            ch_v.n += 1
            ch_v.wait(v)
            for h in range(NH):
                v.max(out=sc8[:, h, :], in_=allsc[:, h, :]).then_inc(ch_v.h, 1)
                ch_v.n += 1
                ch_v.wait(v)
                v.max_index(out=pk8[:, h, :], in_max=sc8[:, h, :],
                            in_values=allsc[:, h, :]).then_inc(ch_v.h, 1)
                ch_v.n += 1
            ch_v.wait(v)
            v.engine_nop()
            v.tensor_scalar(out=pkd[:].rearrange("p h k -> p (h k)"),
                            in0=pk8[:].rearrange("p h k -> p (h k)"),
                            scalar1=3, scalar2=None,
                            op0=AluOpType.logical_shift_right).then_inc(ch_v.h, 1)
            ch_v.n += 1
            v.tensor_scalar(out=pkm[:].rearrange("p h k -> p (h k)"),
                            in0=pk8[:].rearrange("p h k -> p (h k)"),
                            scalar1=7, scalar2=None, op0=AluOpType.bitwise_and
                            ).then_inc(ch_v.h, 1)
            ch_v.n += 1
            ch_v.wait(v)
            v.tensor_tensor(
                out=tsel[:],
                in0=pkd[:][:, :, :, None].to_broadcast([128, NH, KP, KP]),
                in1=iotaj[:][:, None, None, :].to_broadcast([128, NH, KP, KP]),
                op=AluOpType.is_equal).then_inc(ch_v.h, 1)
            ch_v.n += 1
            ch_v.wait(v)
            v.tensor_tensor(
                out=tsel[:], in0=tsel[:],
                in1=ix01[:, 0, :].rearrange("p (h j) -> p h j", j=KP)
                    [:, :, None, :].to_broadcast([128, NH, KP, KP]),
                op=AluOpType.mult).then_inc(ch_v.h, 1)
            ch_v.n += 1
            ch_v.wait(v)
            v.tensor_reduce(out=exK[:], in_=tsel[:], axis=AX.X, op=AluOpType.add
                            ).then_inc(ch_v.h, 1)
            ch_v.n += 1
            ch_v.wait(v)
            v.tensor_tensor(
                out=tsel[:],
                in0=pkm[:][:, :, :, None].to_broadcast([128, NH, KP, KP]),
                in1=iotaj[:][:, None, None, :].to_broadcast([128, NH, KP, KP]),
                op=AluOpType.is_equal).then_inc(ch_v.h, 1)
            ch_v.n += 1
            ch_v.wait(v)
            v.tensor_tensor(
                out=tsel[:], in0=tsel[:],
                in1=ix01[:, 1, :].rearrange("p (h j) -> p h j", j=KP)
                    [:, :, None, :].to_broadcast([128, NH, KP, KP]),
                op=AluOpType.mult).then_inc(ch_v.h, 1)
            ch_v.n += 1
            ch_v.wait(v)
            v.tensor_reduce(out=eyK[:], in_=tsel[:], axis=AX.X, op=AluOpType.add
                            ).then_inc(ch_v.h, 1)
            ch_v.n += 1
            ch_v.wait(v)
            v.tensor_scalar(out=exK[:], in0=exK[:], scalar1=128.0, scalar2=None,
                            op0=AluOpType.mult).then_inc(ch_v.h, 1)
            ch_v.n += 1
            ch_v.wait(v)
            v.tensor_tensor(out=E_all[:, t, :].rearrange("p (h k) -> p h k", k=KP),
                            in0=exK[:], in1=eyK[:], op=AluOpType.add
                            ).then_inc(ch_v.h, 1)
            ch_v.n += 1
            v.tensor_scalar(out=negg[:], in0=sc8[:, :, 0], scalar1=-1.0,
                            scalar2=None, op0=AluOpType.mult).then_inc(s_v.h, 1)
        s_v.n += 1

        @blk.scalar
        def _(sc, t=t):
            s_v.wait(sc)
            for h in range(NH):
                sc.activation(out=gex[:, h, :], in_=sc8[:, h, :], func=AF.Exp,
                              bias=negg[:, h:h + 1]).then_inc(s_sc.h, 1)
        s_sc.n += NH

        @blk.vector
        def _(v, t=t):
            s_sc.wait(v)
            v.tensor_reduce(out=gsum[:], in_=gex[:], axis=AX.X, op=AluOpType.add
                            ).then_inc(ch_v.h, 1)
            ch_v.n += 1
            ch_v.wait(v)
            v.reciprocal(out=gsum[:], in_=gsum[:]).then_inc(ch_v.h, 1)
            ch_v.n += 1
            ch_v.wait(v)
            v.tensor_tensor(out=G_all[:, t, :].rearrange("p (h k) -> p h k", k=KP),
                            in0=gex[:],
                            in1=gsum[:][:, :, None].to_broadcast([128, NH, KP]),
                            op=AluOpType.mult).then_inc(s_v.h, 1)
        s_v.n += 1


        topk_done[t] = s_v.n

    barrier()
    esE.close()


    esG = ExitStack()
    comb_g = sb(esG, "comb_g", [128, 16, DP + D], BF16)
    prodd = sb(esG, "prodd", [128, 4, DP], F32)
    zdot = sb(esG, "zdot", [128, NSLOT], F32)
    zsil = sb(esG, "zsil", [128, NSLOT], F32)
    Dt = sb(esG, "Dt", [128, 8, 128], BF16)
    rload = sb(esG, "rload", [128, 2, D], F32)
    outb = sb(esG, "outb", [128, 2, D], F32)

    @blk.gpsimd
    def _(g):
        g.wait_ge(s_sgout[0].h, s_sgout[0].n)
        g.wait_ge(s_sgout[1].h, s_sgout[1].n)

    mm_after = {}
    dbuild_after = {}
    fin_after = {}

    emit_topk(0)
    for t in range(NT):
        @blk.sync
        def _(sy, t=t):
            sy.dma_start(out=rload[:, t % 2, :],
                         in_=dpark[t * 128:(t + 1) * 128, :]
                         ).then_inc(s_r[t % 2].h, 16)
        s_r[t % 2].n += 16

        for g8 in range(NSLOT // 8):
            s0 = g8 * 8
            if g8 == 2 and t + 1 < NT:
                emit_topk(t + 1)

            @blk.gpsimd
            def _(g, t=t, s0=s0):
                g.wait_ge(s_v.h, topk_done[t])
                for s in range(s0, s0 + 8):
                    gslot = t * NSLOT + s
                    if gslot >= 16:
                        s_vde.wait(g, gslot - 15)
                        g.wait_ge(s_pe.h, mm_after[gslot - 16])
                    g.indirect_dma_start(
                        out=comb_g[:, s % 16, :], out_offset=None, in_=d_tab_bf[:],
                        in_offset=IOffs(E_all[:, t, s:s + 1], 0),
                    ).then_inc(s_de[s % 16].h, 16)
                    s_de[s % 16].n += 16

            @blk.vector
            def _(v, t=t, s0=s0):
                for s in range(s0, s0 + 8):
                    gslot = t * NSLOT + s
                    s_de[s % 16].wait(v)
                    if gslot >= 4:
                        v.wait_ge(s_vde.h, gslot - 3)
                    v.scalar_tensor_tensor(out=prodd[:, s % 4, :],
                                           in0=comb_g[:, s % 16, 0:DP],
                                           scalar=1.0, in1=h2tok[:, t, :],
                                           op0=AluOpType.mult, op1=AluOpType.mult,
                                           accum_out=zdot[:, s:s + 1]
                                           ).then_inc(s_vde.h, 1)
                    s_vde.n += 1
                v.wait_ge(s_vde.h, s_vde.n)
                v.tensor_tensor(out=zsil[:, s0:s0 + 8],
                                in0=zdot[:, s0:s0 + 8],
                                in1=G_all[:, t, s0:s0 + 8],
                                op=AluOpType.mult).then_inc(s_v.h, 1)
            s_v.n += 1

            @blk.scalar
            def _(sc, t=t, s0=s0):
                s_v.wait(sc)
                sc.activation(out=zsil[:, s0:s0 + 8], in_=zsil[:, s0:s0 + 8],
                              func=AF.Silu).then_inc(s_sc.h, 1)
                sc.wait_ge(s_sc.h, s_sc.n + 1)
                for s in range(s0, s0 + 8):
                    gsl = t * NSLOT + s
                    if gsl >= 8:
                        sc.wait_ge(s_pe.h, mm_after[gsl - 8])
                    sc.activation(out=Dt[:, s % 8, :], in_=ident_f[:],
                                  func=AF.Identity, scale=zsil[:, s:s + 1]
                                  ).then_inc(s_sc.h, 1)
                    dbuild_after[gsl] = s_sc.n + 2 + (s - s0)
            s_sc.n += 9

            @blk.tensor
            def _(t_, t=t, s0=s0):
                for s in range(s0, s0 + 8):
                    gsl = t * NSLOT + s
                    t_.wait_ge(s_sc.h, dbuild_after[gsl])
                    t_.wait_ge(s_de[s % 16].h, s_de[s % 16].n)
                    if s == 0 and t > 0:
                        t_.wait_ge(s_v.h, fin_after[t - 1])
                    for k in range(4):
                        ins = t_.matmul(
                            ps_big[:, k * 512:(k + 1) * 512],
                            Dt[:, s % 8, :],
                            comb_g[:, s % 16, DP + k * 512:DP + (k + 1) * 512],
                            start=(s == 0), stop=(s == NSLOT - 1))
                    ins.then_inc(s_pe.h, 1)
                    s_pe.n += 1
                    mm_after[gsl] = s_pe.n

        @blk.vector
        def _(v, t=t):
            s_r[t % 2].wait(v)
            v.wait_ge(s_pe.h, mm_after[t * NSLOT + NSLOT - 1])
            if t >= 2:
                v.wait_ge(s_out2[t % 2].h, 16 * (t // 2))
            v.tensor_tensor(out=outb[:, t % 2, :], in0=ps_big[:, 0:2048],
                            in1=rload[:, t % 2, :], op=AluOpType.add
                            ).then_inc(s_v.h, 1)
        s_v.n += 1
        fin_after[t] = s_v.n

        @blk.sync
        def _(sy, t=t):
            s_v.wait(sy)
            sy.dma_start(out=dout[t * 128:(t + 1) * 128, :], in_=outb[:, t % 2, :]
                         ).then_inc(s_out2[t % 2].h, 16)
        s_out2[t % 2].n += 16

    @blk.sync
    def _(sy):
        sy.wait_ge(s_out2[0].h, s_out2[0].n)
        sy.wait_ge(s_out2[1].h, s_out2[1].n)

    esG.close()
    esH.close()
    es0.close()
    return nc


def kernel(L0_attn_x, L0_attn_w_q, L0_attn_w_kv, L0_attn_w_dense,
           L0_ffn_W_up, L0_ffn_b_up, L0_ffn_W_down, L0_ffn_b_down,
           L0_ffn_W_q, L0_ffn_keys, L0_ffn_down_embed, L0_ffn_up_embed):
    x = np.ascontiguousarray(np.asarray(L0_attn_x, dtype=np.float32))
    weights = {
        "w_q": np.ascontiguousarray(np.asarray(L0_attn_w_q, np.float32).reshape(16, 128, 16, 128).transpose(2, 1, 0, 3)),
        "w_kv": np.ascontiguousarray(np.asarray(L0_attn_w_kv, np.float32).reshape(16, 128, 8, 128).transpose(2, 1, 0, 3)),
        "w_dense": np.ascontiguousarray(np.asarray(L0_attn_w_dense, np.float32).reshape(16, 128, 16, 128).transpose(2, 1, 0, 3)),
        "w_up": np.ascontiguousarray(np.asarray(L0_ffn_W_up, np.float32).T.reshape(16, 128, 64, 128).transpose(2, 1, 0, 3)),
        "b_up": np.ascontiguousarray(np.asarray(L0_ffn_b_up, np.float32)),
        "w_down": np.ascontiguousarray(np.asarray(L0_ffn_W_down, np.float32).T.reshape(8, 8, 128, 8, 128).transpose(3, 0, 2, 1, 4)),
        "b_down": np.ascontiguousarray(np.asarray(L0_ffn_b_down, np.float32)),
        "w_qr": np.ascontiguousarray(np.asarray(L0_ffn_W_q, np.float32).T),
        "keys": np.ascontiguousarray(np.asarray(L0_ffn_keys, np.float32)),
        "tab": np.ascontiguousarray(np.concatenate(
            [np.asarray(L0_ffn_down_embed, np.float32),
             np.asarray(L0_ffn_up_embed, np.float32)], axis=1)),
    }

    nc = build_nc()
    in_maps = []
    tiles_per_core = []
    for c in range(8):
        b = c // 2
        tiles = TILES_EVEN if c % 2 == 0 else TILES_ODD
        tiles_per_core.append(tiles)
        rows = np.concatenate([np.arange(g * 128, (g + 1) * 128) for g in tiles])
        qidx = np.empty((128, NT), np.float32)
        for i, g in enumerate(tiles):
            qidx[:, i] = g * 128 + np.arange(128)
        m = {"x_own": np.ascontiguousarray(x[b][rows]), "x_seq": x[b],
             "qidx": qidx}
        m.update(weights)
        in_maps.append(m)

    res = run_bass_kernel_spmd(nc, in_maps, core_ids=list(range(8)))

    out = np.empty((B, SEQ, D), np.float32)
    for c in range(8):
        tiles = tiles_per_core[c]
        o = res.results[c]["out"]
        b = c // 2
        for i, g in enumerate(tiles):
            out[b, g * 128:(g + 1) * 128, :] = o[i * 128:(i + 1) * 128, :]
    return out


# version 29 (final)
# speedup vs baseline: 1.0629x; 1.0629x over previous
import numpy as np
from contextlib import ExitStack

import concourse.bass as bass
import concourse.mybir as mybir
from concourse.alu_op_type import AluOpType
from concourse.bass_utils import run_bass_kernel_spmd

F32 = mybir.dt.float32
BF16 = mybir.dt.bfloat16
U32 = mybir.dt.uint32
AF = mybir.ActivationFunctionType
AX = mybir.AxisListType
IOffs = bass.IndirectOffsetOnAxis

B = 4
SEQ = 2048
D = 2048
DCH = D // 128
NT = 8
TOK = NT * 128
HEADS = 16
GR = 4
HPG = 4
HD = 128
SCALE = float(1.0 / np.sqrt(HD))
NBLK = [1, 1, 2, 2, 3, 3, 4, 4]
TILES_EVEN = [0, 3, 4, 7, 8, 11, 12, 15]
TILES_ODD = [1, 2, 5, 6, 9, 10, 13, 14]

DC = 8192
NCC = DC // 128
DP = 1024
NPC = DP // 128
DR = 1024
NRC = DR // 128
NE = 16384
NH = 8
KP = 8
NSLOT = NH * KP
MASK_NEG = -1.0e30


class Sem:
    def __init__(self, es, nc, name):
        self.h = es.enter_context(nc.semaphore(name))
        self.n = 0

    def wait(self, eng, v=None):
        eng.wait_ge(self.h, self.n if v is None else v)


def build_nc():
    nc = bass.Bass("TRN2", target_bir_lowering=False)

    dx_own = nc.dram_tensor("x_own", [TOK, D], F32, kind="ExternalInput").ap()
    dx_seq = nc.dram_tensor("x_seq", [SEQ, D], F32, kind="ExternalInput").ap()
    dqidx = nc.dram_tensor("qidx", [128, NT], F32, kind="ExternalInput").ap()
    dw_q = nc.dram_tensor("w_q", [HEADS, 128, DCH, 128], F32, kind="ExternalInput").ap()
    dw_kv = nc.dram_tensor("w_kv", [8, 128, DCH, 128], F32, kind="ExternalInput").ap()
    dw_de = nc.dram_tensor("w_dense", [DCH, 128, DCH, 128], F32, kind="ExternalInput").ap()
    dw_up = nc.dram_tensor("w_up", [NCC, 128, DCH, 128], F32, kind="ExternalInput").ap()
    db_up = nc.dram_tensor("b_up", [DC], F32, kind="ExternalInput").ap()
    dw_dn = nc.dram_tensor("w_down", [NPC, 8, 128, 8, 128], F32, kind="ExternalInput").ap()
    db_dn = nc.dram_tensor("b_down", [DP], F32, kind="ExternalInput").ap()
    dw_qr = nc.dram_tensor("w_qr", [DP, DR], F32, kind="ExternalInput").ap()
    dkeys = nc.dram_tensor("keys", [NH, 128, 2, 64], F32, kind="ExternalInput").ap()
    d_tab = nc.dram_tensor("tab", [NE, DP + D], F32, kind="ExternalInput").ap()
    d_tab_bf = nc.dram_tensor("tab_bf", [NE, DP + D], BF16, kind="Internal").ap()
    dout = nc.dram_tensor("out", [TOK, D], F32, kind="ExternalOutput").ap()
    dpark = nc.dram_tensor("r_park", [TOK, D], F32, kind="Internal").ap()

    es0 = ExitStack()

    def sb(esx, name, shape, dt, side=None):
        if side is None:
            return esx.enter_context(nc.sbuf_tensor(name, shape, dt))
        return esx.enter_context(nc.sbuf_tensor(name, shape, dt, side=side))

    def psum(esx, name, shape, dt):
        return esx.enter_context(nc.psum_tensor(name, shape, dt))

    ident_f = sb(es0, "ident_f", [128, 128], F32)
    ident_b = sb(es0, "ident_b", [128, 128], BF16)
    iota_cr = sb(es0, "iota_cr", [128, 128], F32)
    qidxT = sb(es0, "qidxT", [128, NT], F32)
    keysT = sb(es0, "keysT", [128, 2, NH, 128], BF16)
    b_upT = sb(es0, "b_upT", [128, NCC], F32)
    b_dnT = sb(es0, "b_dnT", [128, NPC], F32)
    epsT = sb(es0, "epsT", [128, 1], F32)
    stat6 = sb(es0, "stat6", [128, 4, 6], F32)
    meanvar = sb(es0, "meanvar", [128, 2], F32)
    negmean = sb(es0, "negmean", [128, 1], F32)
    invstd = sb(es0, "invstd", [128, 1], F32)
    nr_t = sb(es0, "nr_t", [128, 1], F32)
    mx1 = sb(es0, "mx1", [128, 1], F32)
    sum1 = sb(es0, "sum1", [128, 2], F32)
    negsc = sb(es0, "negsc", [128, 2], F32)

    E_all = sb(es0, "E_all", [128, NT, NSLOT], U32)
    G_all = sb(es0, "G_all", [128, NT, NSLOT], F32)
    mx01 = sb(es0, "mx01", [128, 2, NSLOT], F32, side="right")
    ix01 = sb(es0, "ix01", [128, 2, NSLOT], U32, side="right")
    allsc = sb(es0, "allsc", [128, NH, 64], F32, side="right")
    sc8 = sb(es0, "sc8", [128, NH, KP], F32, side="right")
    pk8 = sb(es0, "pk8", [128, NH, KP], U32, side="right")
    pkd = sb(es0, "pkd", [128, NH, KP], U32, side="right")
    pkm = sb(es0, "pkm", [128, NH, KP], U32, side="right")
    iotaj = sb(es0, "iotaj", [128, KP], F32)
    tsel = sb(es0, "tsel", [128, NH, KP, KP], F32, side="right")
    exK = sb(es0, "exK", [128, NH, KP], F32, side="right")
    eyK = sb(es0, "eyK", [128, NH, KP], F32, side="right")
    gex = sb(es0, "gex", [128, NH, KP], F32, side="right")
    gsum = sb(es0, "gsum", [128, NH], F32)
    negg = sb(es0, "negg", [128, NH], F32)

    ps_big = psum(es0, "ps_big", [128, 2048], F32)
    ps_a = psum(es0, "ps_a", [128, 512], F32)
    ps_c = psum(es0, "ps_c", [128, 512], F32)
    ps_bf = psum(es0, "ps_bf", [128, 2048], BF16)

    S = lambda name: Sem(es0, nc, name)
    s_ld = S("s_ld")
    s_ldk = S("s_ldk")
    s_ldx = [S(f"s_ldx{i}") for i in range(NT)]
    s_ldb = [S(f"s_ldb{i}") for i in range(4)]
    s_v = S("s_v")
    s_sc = S("s_sc")
    s_pe = S("s_pe")
    s_g = S("s_g")
    s_ev = S("s_ev")
    s_bar = S("s_bar")
    s_park = S("s_park")
    s_out2 = [S(f"s_out{i}") for i in range(2)]
    s_de = [S(f"s_de{i}") for i in range(16)]
    s_ue = [S(f"s_ue{i}") for i in range(4)]
    s_vde = S("s_vde")
    s_vue = S("s_vue")
    s_zg = S("s_zg")
    s_r = [S(f"s_r{i}") for i in range(2)]
    ch_v = S("ch_v")
    ch_g = S("ch_g")

    s_sgin = [S(f"s_sgin{i}") for i in range(2)]
    s_sgout = [S(f"s_sgout{i}") for i in range(2)]
    s_sgc = S("s_sgc")
    esStg = ExitStack()
    stg_f = sb(esStg, "stg_f", [128, 2, 3072], F32, side="right")
    stg_b = sb(esStg, "stg_b", [128, 2, 3072], BF16, side="right")
    stage_state = {"i": 0}
    N_STG = NE // 128

    blk = es0.enter_context(nc.Block())

    def emit_stage_chunk(n=1):
        n = min(n, N_STG - stage_state["i"])
        if n <= 0:
            return
        chunks = []
        for _ in range(n):
            i = stage_state["i"]
            stage_state["i"] = i + 1
            chunks.append((i, i % 2))

        @blk.gpsimd
        def _(g, chunks=chunks):
            for i, buf in chunks:
                if i >= 2:
                    g.wait_ge(s_sgc.h, i - 1)
                g.dma_start(out=stg_f[:, buf, :],
                            in_=d_tab[i * 128:(i + 1) * 128, :]
                            ).then_inc(s_sgin[buf].h, 16)

        @blk.scalar
        def _(sc, chunks=chunks):
            for i, buf in chunks:
                sc.wait_ge(s_sgin[buf].h, 16 * (i // 2 + 1))
                if i >= 2:
                    sc.wait_ge(s_sgout[buf].h, 16 * (i // 2))
                sc.activation(out=stg_b[:, buf, :], in_=stg_f[:, buf, :],
                              func=AF.Copy).then_inc(s_sgc.h, 1)
                sc.wait_ge(s_sgc.h, i + 1)
                sc.dma_start(out=d_tab_bf[i * 128:(i + 1) * 128, :],
                             in_=stg_b[:, buf, :]).then_inc(s_sgout[buf].h, 16)

        for i, buf in chunks:
            s_sgin[buf].n += 16
            s_sgc.n += 1
            s_sgout[buf].n += 16

    def barrier(light=False):
        decs = ((blk.sync, blk.tensor, blk.vector) if light else
                (blk.sync, blk.tensor, blk.vector, blk.scalar, blk.gpsimd))
        for dec in decs:
            @dec
            def _(e):
                e.drain()
                e.sem_inc(s_bar.h, 1)
        s_bar.n += len(decs)
        for dec in decs:
            @dec
            def _(e):
                s_bar.wait(e)

    esInit = ExitStack()
    bload = sb(esInit, "bload", [64, 128], F32)
    bload2 = sb(esInit, "bload2", [8, 128], F32)

    @blk.sync
    def _(sy):
        sy.dma_start(out=qidxT[:], in_=dqidx[:]).then_inc(s_ld.h, 16)
        sy.dma_start(out=bload[:],
                     in_=db_up[:].rearrange("(a p) -> a p", a=64)
                     ).then_inc(s_ld.h, 16)
        sy.dma_start(out=bload2[:],
                     in_=db_dn[:].rearrange("(a p) -> a p", a=8)
                     ).then_inc(s_ld.h, 16)
    s_ld.n += 48

    @blk.tensor
    def _(t):
        s_v.wait(t, 1)
        s_ld.wait(t, 48)
        t.transpose(ps_big[:, 0:64], bload[:], ident_f[0:64, 0:64]
                    ).then_inc(s_pe.h, 1)
        t.transpose(ps_big[:, 512:512 + 8], bload2[:], ident_f[0:8, 0:8]
                    ).then_inc(s_pe.h, 1)
    s_pe.n += 2

    @blk.scalar
    def _(sc):
        s_pe.wait(sc)
        sc.activation(out=b_upT[:], in_=ps_big[:, 0:64], func=AF.Copy
                      ).then_inc(s_sc.h, 1)
        sc.activation(out=b_dnT[:], in_=ps_big[:, 512:512 + 8], func=AF.Copy
                      ).then_inc(s_sc.h, 1)
    s_sc.n += 2

    @blk.gpsimd
    def _(g):
        g.iota(iota_cr[:], pattern=[[1, 128]], base=0, channel_multiplier=-1,
               allow_small_or_imprecise_dtypes=True)
        g.iota(iotaj[:], pattern=[[1, KP]], base=0, channel_multiplier=0,
               allow_small_or_imprecise_dtypes=True).then_inc(s_g.h, 1)
    s_g.n += 1

    @blk.vector
    def _(v):
        v.memset(epsT[:], 1e-5)
        s_g.wait(v)
        v.tensor_scalar(out=ident_f[:], in0=iota_cr[:], scalar1=0.0, scalar2=None,
                        op0=AluOpType.is_equal).then_inc(ch_v.h, 1)
        ch_v.n += 1
        ch_v.wait(v)
        v.tensor_copy(out=ident_b[:], in_=ident_f[:]).then_inc(s_v.h, 1)
    s_v.n += 1

    kbuf = sb(esInit, "kbuf", [128, 64], F32)
    kbufb = sb(esInit, "kbufb", [128, 64], BF16)
    for h in range(NH):
        for p in range(2):
            @blk.sync
            def _(sy, h=h, p=p):
                s_sc.wait(sy)
                sy.dma_start(out=kbuf[:], in_=dkeys[h, :, p, :]).then_inc(s_ldk.h, 16)
            s_ldk.n += 16

            @blk.vector
            def _(v):
                s_ldk.wait(v)
                s_pe.wait(v)
                v.tensor_copy(out=kbufb[:], in_=kbuf[:]).then_inc(s_v.h, 1)
            s_v.n += 1

            @blk.tensor
            def _(t):
                s_v.wait(t)
                t.transpose(ps_bf[0:64, 0:128], kbufb[:], ident_b[:]
                            ).then_inc(s_pe.h, 1)
            s_pe.n += 1

            @blk.scalar
            def _(sc, h=h, p=p):
                s_pe.wait(sc)
                sc.activation(out=keysT[0:64, p, h, :], in_=ps_bf[0:64, 0:128],
                              func=AF.Copy).then_inc(s_sc.h, 1)
                sc.activation(out=keysT[64:128, p, h, :], in_=ps_bf[0:64, 0:128],
                              func=AF.Copy).then_inc(s_sc.h, 1)
            s_sc.n += 2

    def emit_ln(src_ap, dst_ap):
        @blk.vector
        def _(v):
            for i in range(4):
                v.bn_stats(out=stat6[:, i, :],
                           in_=src_ap[:, 512 * i:512 * (i + 1)]
                           ).then_inc(ch_v.h, 1)
                ch_v.n += 1
            ch_v.wait(v)
            v.bn_aggr(out=meanvar[:], in_=stat6[:].rearrange("p a b -> p (a b)")
                      ).then_inc(ch_v.h, 1)
            ch_v.n += 1
            ch_v.wait(v)
            v.tensor_scalar(out=negmean[:], in0=meanvar[:, 0:1], scalar1=-1.0,
                            scalar2=None, op0=AluOpType.mult).then_inc(s_v.h, 1)
        s_v.n += 1

        @blk.scalar
        def _(sc):
            s_v.wait(sc)
            sc.activation(out=invstd[:], in_=meanvar[:, 1:2], func=AF.Sqrt,
                          bias=epsT[:]).then_inc(s_sc.h, 1)
        s_sc.n += 1

        @blk.vector
        def _(v):
            s_sc.wait(v)
            s_pe.wait(v)
            v.reciprocal(out=invstd[:], in_=invstd[:]).then_inc(ch_v.h, 1)
            ch_v.n += 1
            ch_v.wait(v)
            v.tensor_tensor(out=nr_t[:], in0=invstd[:], in1=invstd[:],
                            op=AluOpType.mult).then_inc(ch_v.h, 1)
            ch_v.n += 1
            ch_v.wait(v)
            v.tensor_scalar(out=nr_t[:], in0=meanvar[:, 1:2], scalar1=nr_t[:],
                            scalar2=-0.5, op0=AluOpType.mult, op1=AluOpType.mult
                            ).then_inc(ch_v.h, 1)
            ch_v.n += 1
            ch_v.wait(v)
            v.tensor_scalar(out=nr_t[:], in0=nr_t[:], scalar1=1.5, scalar2=None,
                            op0=AluOpType.add).then_inc(ch_v.h, 1)
            ch_v.n += 1
            ch_v.wait(v)
            v.tensor_tensor(out=invstd[:], in0=invstd[:], in1=nr_t[:],
                            op=AluOpType.mult).then_inc(ch_v.h, 1)
            ch_v.n += 1
            ch_v.wait(v)
            v.tensor_scalar(out=dst_ap, in0=src_ap, scalar1=negmean[:],
                            scalar2=invstd[:], op0=AluOpType.add, op1=AluOpType.mult
                            ).then_inc(s_v.h, 1)
        s_v.n += 1

    def emit_transpose16(src_fn, dst_fn, n_chunks=16, ident=None):
        ident = ident_b[:] if ident is None else ident
        for g4 in range(0, n_chunks, 4):
            cnt = min(4, n_chunks - g4)

            @blk.tensor
            def _(t, g4=g4, cnt=cnt):
                s_ev.wait(t)
                s_v.wait(t)
                for j in range(cnt):
                    ins = t.transpose(ps_bf[:, j * 256:j * 256 + 128],
                                      src_fn(g4 + j), ident)
                    if j == cnt - 1:
                        ins.then_inc(s_pe.h, 1)
            s_pe.n += 1

            @blk.vector
            def _(v, g4=g4, cnt=cnt):
                s_pe.wait(v)
                for j in range(cnt):
                    v.tensor_copy(out=dst_fn(g4 + j),
                                  in_=ps_bf[:, j * 256:j * 256 + 128]
                                  ).then_inc(s_ev.h, 1)
            s_ev.n += cnt

    barrier()
    esInit.close()
    esA = ExitStack()
    esR = ExitStack()
    R = sb(esR, "R", [128, NT, D], F32, side="right")
    xn_bf = sb(esA, "xn_bf", [128, D], BF16)
    esQT = ExitStack()
    q_T = sb(esQT, "q_T", [128, HEADS, TOK], BF16)
    esQ = ExitStack()
    xnT_own = sb(esQ, "xnT_own", [128, DCH, TOK], BF16)

    for t in range(NT):
        @blk.sync
        def _(sy, t=t):
            sy.dma_start(out=R[:, t, :], in_=dx_own[t * 128:(t + 1) * 128, :]
                         ).then_inc(s_ldx[t].h, 16)
        s_ldx[t].n += 16

        @blk.vector
        def _(v, t=t):
            v.wait_ge(s_ldx[t].h, 16)
        emit_ln(R[:, t, :], xn_bf[:])
        emit_transpose16(lambda c: xn_bf[:, c * 128:(c + 1) * 128],
                         lambda c, t=t: xnT_own[:, c, t * 128:(t + 1) * 128])
        emit_stage_chunk(1)

    wqe = sb(esQ, "wqe", [128, 1, DCH, 128], BF16)
    wqe_f = sb(esQ, "wqe_f", [128, DCH, 128], F32)

    def emit_proj_stream(w_dram, n_et, rhs_fn, out_fn):
        for et in range(n_et):
            buf = et % 2

            @blk.sync
            def _(sy, et=et, buf=buf):
                s_v.wait(sy, s_v.n)
                if et >= 2:
                    s_pe.wait(sy, s_pe.n)
                sy.dma_start(out=wqe_f[:], in_=w_dram[et]
                             ).then_inc(s_ldb[buf].h, 16)
            s_ldb[buf].n += 16

            @blk.vector
            def _(v, buf=buf):
                s_ldb[buf].wait(v)
                s_pe.wait(v, s_pe.n)
                v.tensor_copy(out=wqe[:, 0, :, :], in_=wqe_f[:, :, :]).then_inc(s_v.h, 1)
            s_v.n += 1

            for nt in range(2):
                pst = ps_a if nt == 0 else ps_c

                @blk.tensor
                def _(t, et=et, buf=buf, nt=nt, pst=pst):
                    s_v.wait(t)
                    s_ev.wait(t)
                    for dt in range(DCH):
                        ins = t.matmul(pst[:], wqe[:, 0, dt, :], rhs_fn(dt, nt),
                                       start=(dt == 0), stop=(dt == DCH - 1))
                    ins.then_inc(s_pe.h, 1)
                s_pe.n += 1

                @blk.vector
                def _(v, et=et, nt=nt, pst=pst):
                    s_pe.wait(v)
                    v.tensor_scalar(out=out_fn(et, nt), in0=pst[:], scalar1=SCALE,
                                    scalar2=None, op0=AluOpType.mult
                                    ).then_inc(s_ev.h, 1)
                s_ev.n += 1

    emit_proj_stream(dw_q, HEADS,
                     lambda dt, nt: xnT_own[:, dt, nt * 512:(nt + 1) * 512],
                     lambda et, nt: q_T[:, et, nt * 512:(nt + 1) * 512])

    barrier(light=True)
    esQ.close()

    @blk.sync
    def _(sy):
        s_v.wait(sy)
        s_ev.wait(sy)
        for t in range(NT):
            sy.dma_start(out=dpark[t * 128:(t + 1) * 128, :], in_=R[:, t, :]
                         ).then_inc(s_park.h, 16)
        sy.wait_ge(s_park.h, s_park.n + 16 * NT)
    s_park.n += 16 * NT
    barrier(light=True)
    esR.close()

    kT = sb(esQT, "kT", [128, GR, SEQ], BF16)
    v_km = sb(esQT, "v_km", [128, GR, SEQ], BF16)

    esA2 = ExitStack()
    xsq = sb(esA2, "xsq", [128, D], F32)
    xn_blk = sb(esA2, "xn_blk", [128, DCH, 512], BF16)
    vT = sb(esA2, "vT", [128, GR, 512], BF16)
    wkv_f = sb(esA2, "wkv_f", [128, DCH, 128], F32)
    wkv_b = sb(esA2, "wkv_b", [128, 2, DCH, 128], BF16)
    kv_cast = {0: 0, 1: 0}
    kv_mm = {0: 0, 1: 0}

    for bidx in range(SEQ // 512):
        for tt in range(4):
            @blk.sync
            def _(sy, bidx=bidx, tt=tt):
                s_v.wait(sy, s_v.n)
                sy.dma_start(out=xsq[:],
                             in_=dx_seq[(bidx * 4 + tt) * 128:
                                        (bidx * 4 + tt + 1) * 128, :]
                             ).then_inc(s_ldb[2].h, 16)
            s_ldb[2].n += 16

            @blk.vector
            def _(v):
                s_ldb[2].wait(v)
            emit_ln(xsq[:], xn_bf[:])
            emit_transpose16(lambda c: xn_bf[:, c * 128:(c + 1) * 128],
                             lambda c, tt=tt:
                             xn_blk[:, c, tt * 128:(tt + 1) * 128])
            emit_stage_chunk(1)

        for et in range(8):
            buf = et % 2
            pst = ps_a if et % 2 == 0 else ps_c
            dst = kT if et < 4 else vT
            g = et % 4
            dcol = bidx * 512 if et < 4 else 0

            @blk.sync
            def _(sy, et=et, buf=buf):
                if kv_cast[0] or kv_cast[1]:
                    sy.wait_ge(s_v.h, max(kv_cast[0], kv_cast[1]))
                sy.dma_start(out=wkv_f[:], in_=dw_kv[et]
                             ).then_inc(s_ldb[3].h, 16)
            s_ldb[3].n += 16

            @blk.vector
            def _(v, et=et, buf=buf):
                s_ldb[3].wait(v)
                if kv_mm[buf]:
                    v.wait_ge(s_pe.h, kv_mm[buf])
                v.tensor_copy(out=wkv_b[:, buf, :, :], in_=wkv_f[:, :, :]
                              ).then_inc(s_v.h, 1)
            s_v.n += 1
            kv_cast[buf] = s_v.n

            @blk.tensor
            def _(t, et=et, buf=buf, pst=pst):
                t.wait_ge(s_v.h, kv_cast[buf])
                s_ev.wait(t)
                for dt in range(DCH):
                    ins = t.matmul(pst[:], wkv_b[:, buf, dt, :],
                                   xn_blk[:, dt, :],
                                   start=(dt == 0), stop=(dt == DCH - 1))
                ins.then_inc(s_pe.h, 1)
            s_pe.n += 1
            kv_mm[buf] = s_pe.n

            @blk.vector
            def _(v, bidx=bidx, g=g, dst=dst, pst=pst):
                s_pe.wait(v)
                v.tensor_copy(out=dst[:, g, dcol:dcol + 512], in_=pst[:]
                              ).then_inc(s_ev.h, 1)
            s_ev.n += 1

        emit_transpose16(
            lambda i, bidx=bidx: vT[:, i // 4, (i % 4) * 128:(i % 4) * 128 + 128],
            lambda i, bidx=bidx: v_km[:, i // 4,
                                      bidx * 512 + (i % 4) * 128:
                                      bidx * 512 + (i % 4) * 128 + 128],
            n_chunks=16)

    barrier(light=True)
    esA2.close()

    esB = ExitStack()
    ctx = sb(esB, "ctx", [128, HEADS, TOK], BF16, side="right")
    esB1 = ExitStack()
    maskb_all = sb(esB1, "maskb_all", [128, NT, 512], BF16)
    mios_t = sb(esB1, "mios", [128, 512], F32)
    mios = mios_t[:]
    P = sb(esB1, "P", [128, 2, 2048], BF16)
    P_T = sb(esB1, "P_T", [128, 16, 128], BF16)

    for slot in range(NT):
        @blk.gpsimd
        def _(g, slot=slot):
            if slot > 0:
                s_v.wait(g, s_v.n)
            else:
                g.wait_ge(s_pe.h, s_pe.n)
                g.wait_ge(s_v.h, s_v.n)
            g.iota(mios, pattern=[[1, 512]], base=(NBLK[slot] - 1) * 512,
                   channel_multiplier=0, allow_small_or_imprecise_dtypes=True
                   ).then_inc(s_g.h, 1)
        s_g.n += 1

        @blk.vector
        def _(v, slot=slot):
            s_g.wait(v)
            v.tensor_scalar(out=mios, in0=mios,
                            scalar1=qidxT[:, slot:slot + 1],
                            scalar2=0.0, op0=AluOpType.subtract, op1=AluOpType.max
                            ).then_inc(ch_v.h, 1)
            ch_v.n += 1
            ch_v.wait(v)
            v.tensor_scalar(out=maskb_all[:, slot, :], in0=mios, scalar1=MASK_NEG,
                            scalar2=None, op0=AluOpType.mult).then_inc(s_v.h, 1)
        s_v.n += 1

    pairs = [(s_, h_) for s_ in range(NT) for h_ in range(HEADS)]
    exp_done = {}
    trans_done = {}
    scale_done = {}
    evac_done = {}
    ctxev_done = {}
    half_state = {"g": 0}

    def stage_a(i):
        slot, h = pairs[i]
        nb = NBLK[slot]
        nk = nb * 512
        grp = h // HPG
        par = i % 2

        sb0 = (i % 2) * 1024 if nb <= 2 else 0

        @blk.tensor
        def _(t, slot=slot, h=h, grp=grp, nb=nb, i=i, sb0=sb0):
            if nb <= 2:
                if i >= 2:
                    t.wait_ge(s_sc.h, exp_done[i - 2])
            elif i >= 1:
                t.wait_ge(s_sc.h, exp_done[i - 1])
            s_v.wait(t)
            for b in range(nb):
                t.matmul(ps_big[:, sb0 + b * 512:sb0 + (b + 1) * 512],
                         q_T[:, h, slot * 128:(slot + 1) * 128],
                         kT[:, grp, b * 512:(b + 1) * 512],
                         start=True, stop=(b < nb - 1))
            ins = t.matmul(ps_big[:, sb0 + (nb - 1) * 512:sb0 + nb * 512],
                           ident_b[:], maskb_all[:, slot, :],
                           start=False, stop=True)
            ins.then_inc(s_pe.h, 1)
        s_pe.n += 1

        @blk.vector
        def _(v, nk=nk, par=par, sb0=sb0):
            s_pe.wait(v)
            v.reduce_max(out=negsc[:, par:par + 1], in_=ps_big[:, sb0:sb0 + nk],
                         axis=AX.X, negate=True).then_inc(s_v.h, 1)
        s_v.n += 1

        @blk.scalar
        def _(sc, nk=nk, par=par, i=i, sb0=sb0):
            s_v.wait(sc)
            if i >= 2:
                sc.wait_ge(s_pe.h, trans_done[i - 2])
            sc.activation(out=P[:, par, 0:nk], in_=ps_big[:, sb0:sb0 + nk],
                          func=AF.Exp, bias=negsc[:, par:par + 1]
                          ).then_inc(s_sc.h, 1)
        s_sc.n += 1
        exp_done[i] = s_sc.n

    def stage_b(i):
        slot, h = pairs[i]
        nb = NBLK[slot]
        nk = nb * 512
        nkt = nb * 4
        grp = h // HPG
        par = i % 2
        ops = ps_c if par == 0 else ps_a

        @blk.vector
        def _(v, nk=nk, par=par, i=i):
            v.wait_ge(s_sc.h, exp_done[i])
            v.reduce_sum(out=sum1[:, par:par + 1], in_=P[:, par, 0:nk], axis=AX.X
                         ).then_inc(ch_v.h, 1)
            ch_v.n += 1
            ch_v.wait(v)
            v.reciprocal(out=sum1[:, par:par + 1], in_=sum1[:, par:par + 1]
                         ).then_inc(ch_v.h, 1)
            ch_v.n += 1
            ch_v.wait(v)
            v.tensor_scalar(out=P[:, par, 0:nk], in0=P[:, par, 0:nk],
                            scalar1=sum1[:, par:par + 1],
                            scalar2=None, op0=AluOpType.mult).then_inc(s_v.h, 1)
        s_v.n += 1
        scale_done[i] = s_v.n

        for g4 in range(0, nkt, 4):
            gh = half_state["g"] % 2
            half_state["g"] += 1
            off = gh * 1024

            @blk.tensor
            def _(t, g4=g4, par=par, i=i, off=off, gh=gh):
                t.wait_ge(s_v.h, scale_done[i])
                if gh in evac_done:
                    t.wait_ge(s_ev.h, evac_done[gh])
                for j in range(4):
                    ins = t.transpose(ps_bf[:, off + j * 256:off + j * 256 + 128],
                                      P[:, par, (g4 + j) * 128:(g4 + j + 1) * 128],
                                      ident_b[:])
                ins.then_inc(s_pe.h, 1)
            s_pe.n += 1

            @blk.vector
            def _(v, g4=g4, off=off):
                s_pe.wait(v)
                for j in range(4):
                    v.tensor_copy(out=P_T[:, g4 + j, :],
                                  in_=ps_bf[:, off + j * 256:off + j * 256 + 128]
                                  ).then_inc(s_ev.h, 1)
            s_ev.n += 4
            evac_done[gh] = s_ev.n

            @blk.tensor
            def _(t, g4=g4, grp=grp, i=i, nkt=nkt, ops=ops):
                s_ev.wait(t)
                if g4 == 0 and i >= 2:
                    t.wait_ge(s_ev.h, ctxev_done[i - 2])
                for j in range(4):
                    kt = g4 + j
                    ins = t.matmul(ops[:, 0:128],
                                   v_km[:, grp, kt * 128:(kt + 1) * 128],
                                   P_T[:, kt, :],
                                   start=(kt == 0), stop=(kt == nkt - 1))
                ins.then_inc(s_pe.h, 1)
            s_pe.n += 1
        trans_done[i] = s_pe.n

        @blk.vector
        def _(v, slot=slot, h=h, ops=ops):
            s_pe.wait(v)
            v.tensor_copy(out=ctx[:, h, slot * 128:(slot + 1) * 128],
                          in_=ops[:, 0:128]).then_inc(s_ev.h, 1)
        s_ev.n += 1
        ctxev_done[i] = s_ev.n
        if i % 4 == 0:
            emit_stage_chunk(2)

    for i in range(len(pairs)):
        stage_a(i)
        if i >= 1:
            stage_b(i - 1)
    stage_b(len(pairs) - 1)

    barrier()
    esB1.close()
    esQT.close()

    wde = sb(esB, "wde", [128, 2, DCH, 128], BF16)
    wde_f = sb(esB, "wde_f", [128, DCH, 128], F32)
    y_f = sb(esB, "y_f", [128, 1024], F32)
    R2 = sb(esB, "R2", [128, NT, D], F32)

    @blk.sync
    def _(sy):
        for t in range(NT):
            sy.dma_start(out=R2[:, t, :], in_=dpark[t * 128:(t + 1) * 128, :]
                         ).then_inc(s_park.h, 16)
    s_park.n += 16 * NT
    _rld_done = s_park.n

    for ht in range(DCH):
        buf = ht % 2

        @blk.sync
        def _(sy, ht=ht, buf=buf):
            s_v.wait(sy, s_v.n)
            if ht >= 2:
                s_pe.wait(sy, s_pe.n)
            sy.dma_start(out=wde_f[:], in_=dw_de[ht]
                         ).then_inc(s_ldb[buf].h, 16)
        s_ldb[buf].n += 16

        @blk.vector
        def _(v, buf=buf):
            s_ldb[buf].wait(v)
            s_pe.wait(v, s_pe.n)
            v.tensor_copy(out=wde[:, buf, :, :], in_=wde_f[:, :, :]).then_inc(s_v.h, 1)
        s_v.n += 1

        for nt in range(2):
            pst = ps_a if nt == 0 else ps_c

            @blk.tensor
            def _(t, buf=buf, nt=nt, pst=pst):
                s_v.wait(t)
                s_ev.wait(t)
                for et in range(DCH):
                    ins = t.matmul(pst[:], wde[:, buf, et, :],
                                   ctx[:, et, nt * 512:(nt + 1) * 512],
                                   start=(et == 0), stop=(et == DCH - 1))
                ins.then_inc(s_pe.h, 1)
            s_pe.n += 1

            @blk.vector
            def _(v, nt=nt, pst=pst):
                s_pe.wait(v)
                v.tensor_copy(out=y_f[:, nt * 512:(nt + 1) * 512], in_=pst[:]
                              ).then_inc(s_ev.h, 1)
            s_ev.n += 1

        for g4 in range(0, 8, 4):
            @blk.tensor
            def _(t, g4=g4):
                s_ev.wait(t)
                s_v.wait(t)
                for j in range(4):
                    ins = t.transpose(ps_big[:, j * 512:j * 512 + 128],
                                      y_f[:, (g4 + j) * 128:(g4 + j + 1) * 128],
                                      ident_f[:])
                ins.then_inc(s_pe.h, 1)
            s_pe.n += 1

            @blk.vector
            def _(v, ht=ht, g4=g4):
                s_pe.wait(v)
                v.wait_ge(s_park.h, _rld_done)
                v.tensor_tensor(
                    out=R2[:, g4:g4 + 4, ht * 128:(ht + 1) * 128],
                    in0=R2[:, g4:g4 + 4, ht * 128:(ht + 1) * 128],
                    in1=ps_big[:].rearrange("p (a b) -> p a b", b=512)[:, :, 0:128],
                    op=AluOpType.add).then_inc(s_v.h, 1)
            s_v.n += 1
        emit_stage_chunk(2)

    barrier()
    esB.close()
    emit_stage_chunk(N_STG - stage_state["i"])

    @blk.sync
    def _(sy):
        sy.wait_ge(s_sgout[0].h, s_sgout[0].n)
        sy.wait_ge(s_sgout[1].h, s_sgout[1].n)
    barrier()
    esStg.close()

    esH = ExitStack()
    h2tok = sb(esH, "h2tok", [128, NT, DP], BF16, side="right")
    qret = sb(esH, "qret", [128, NRC, TOK], BF16, side="right")
    esE = ExitStack()
    xn2T = sb(esE, "xn2T", [128, DCH, TOK], BF16, side="right")
    xn_bf2 = sb(esE, "xn_bf2", [128, D], BF16, side="right")

    for t in range(NT):
        emit_ln(R2[:, t, :], xn_bf2[:])
        emit_transpose16(lambda c: xn_bf2[:, c * 128:(c + 1) * 128],
                         lambda c, t=t: xn2T[:, c, t * 128:(t + 1) * 128])

    @blk.sync
    def _(sy):
        s_v.wait(sy)
        s_ev.wait(sy)
        for t in range(NT):
            sy.dma_start(out=dpark[t * 128:(t + 1) * 128, :], in_=R2[:, t, :]
                         ).then_inc(s_park.h, 16)
        sy.wait_ge(s_park.h, s_park.n + 16 * NT)
    s_park.n += 16 * NT

    barrier()
    esA.close()

    h1 = sb(esE, "h1", [128, NCC, 512], BF16)
    h2fm = sb(esE, "h2fm", [128, NPC, TOK], BF16)
    wuf = sb(esE, "wuf", [128, 2, DCH, 128], F32)
    wub = sb(esE, "wub", [128, 2, DCH, 128], BF16)
    wqrb = sb(esE, "wqrb", [128, NPC, DR], BF16)

    e_cast = {0: 0, 1: 0}
    e_mm = {0: 0, 1: 0}
    e_psum = {0: 0, 1: 0}
    e_ld = {0: 0, 1: 0}

    for pt in range(NPC):
        buf = pt % 2

        @blk.sync
        def _(sy, pt=pt, buf=buf):
            if e_cast[buf]:
                sy.wait_ge(s_v.h, e_cast[buf])
            sy.dma_start(out=wuf[:, buf, 0:8, :],
                         in_=dw_qr[pt * 128:(pt + 1) * 128, :]
                         ).then_inc(s_ldb[buf].h, 16)
        s_ldb[buf].n += 16
        e_ld[buf] = s_ldb[buf].n

        @blk.vector
        def _(v, pt=pt, buf=buf):
            v.wait_ge(s_ldb[buf].h, e_ld[buf])
            v.tensor_copy(out=wqrb[:, pt, :],
                          in_=wuf[:, buf, 0:8, :]).then_inc(s_v.h, 1)
        s_v.n += 1
        e_cast[buf] = s_v.n

    for half in range(2):
        tok0 = half * 512

        for cc in range(NCC):
            buf = cc % 2
            pq = cc % 4
            pst = ps_big[:, pq * 512:(pq + 1) * 512]

            @blk.sync
            def _(sy, cc=cc, buf=buf):
                if e_cast[buf]:
                    sy.wait_ge(s_v.h, e_cast[buf])
                sy.dma_start(out=wuf[:, buf, :, :], in_=dw_up[cc]
                             ).then_inc(s_ldb[buf].h, 16)
            s_ldb[buf].n += 16
            e_ld[buf] = s_ldb[buf].n

            @blk.vector
            def _(v, cc=cc, buf=buf):
                v.wait_ge(s_ldb[buf].h, e_ld[buf])
                if e_mm[buf]:
                    v.wait_ge(s_pe.h, e_mm[buf])
                v.tensor_copy(out=wub[:, buf, :, :], in_=wuf[:, buf, :, :]
                              ).then_inc(s_v.h, 1)
            s_v.n += 1
            e_cast[buf] = s_v.n

            @blk.tensor
            def _(t, cc=cc, buf=buf, pq=pq, pst=pst, tok0=tok0):
                t.wait_ge(s_v.h, e_cast[buf])
                if e_psum.get(pq):
                    t.wait_ge(s_sc.h, e_psum[pq])
                for dt in range(DCH):
                    ins = t.matmul(pst, wub[:, buf, dt, :],
                                   xn2T[:, dt, tok0:tok0 + 512],
                                   start=(dt == 0), stop=(dt == DCH - 1))
                ins.then_inc(s_pe.h, 1)
            s_pe.n += 1
            e_mm[buf] = s_pe.n

            @blk.scalar
            def _(sc, cc=cc, pst=pst):
                s_pe.wait(sc)
                sc.activation(out=h1[:, cc, :], in_=pst, func=AF.Silu,
                              bias=b_upT[:, cc:cc + 1]).then_inc(s_sc.h, 1)
            s_sc.n += 1
            e_psum[pq] = s_sc.n

        barrier()
        e_cast = {0: 0, 1: 0}
        e_mm = {0: 0, 1: 0}
        e_psum = {0: 0, 1: 0}

        for pc in range(NPC):
            dps = ps_c if pc % 2 == 0 else ps_a
            for piece in range(8):
                it = pc * 8 + piece
                buf = it % 2

                @blk.sync
                def _(sy, pc=pc, piece=piece, buf=buf):
                    if e_cast[buf]:
                        sy.wait_ge(s_v.h, e_cast[buf])
                    sy.dma_start(out=wuf[:, buf, 0:8, :], in_=dw_dn[pc, piece]
                                 ).then_inc(s_ldb[buf].h, 16)
                s_ldb[buf].n += 16
                e_ld[buf] = s_ldb[buf].n

                @blk.vector
                def _(v, buf=buf):
                    v.wait_ge(s_ldb[buf].h, e_ld[buf])
                    if e_mm[buf]:
                        v.wait_ge(s_pe.h, e_mm[buf])
                    v.tensor_copy(out=wub[:, buf, 0:8, :], in_=wuf[:, buf, 0:8, :]
                                  ).then_inc(s_v.h, 1)
                s_v.n += 1
                e_cast[buf] = s_v.n

                @blk.tensor
                def _(t, pc=pc, piece=piece, buf=buf, dps=dps):
                    t.wait_ge(s_v.h, e_cast[buf])
                    if piece == 0 and e_psum.get(pc % 2):
                        t.wait_ge(s_sc.h, e_psum[pc % 2])
                    for c8 in range(8):
                        cc = piece * 8 + c8
                        ins = t.matmul(dps[:], wub[:, buf, c8, :], h1[:, cc, :],
                                       start=(cc == 0), stop=(cc == NCC - 1))
                    ins.then_inc(s_pe.h, 1)
                s_pe.n += 1
                e_mm[buf] = s_pe.n

            @blk.scalar
            def _(sc, pc=pc, tok0=tok0, dps=dps):
                s_pe.wait(sc)
                sc.activation(out=h2fm[:, pc, tok0:tok0 + 512], in_=dps[:],
                              func=AF.Identity, bias=b_dnT[:, pc:pc + 1]
                              ).then_inc(s_sc.h, 1)
            s_sc.n += 1
            e_psum[pc % 2] = s_sc.n

        barrier()
        e_cast = {0: 0, 1: 0}
        e_mm = {0: 0, 1: 0}
        e_psum = {0: 0, 1: 0}

        for rt in range(NRC):
            pq = rt % 4
            pst = ps_big[:, pq * 512:(pq + 1) * 512]

            @blk.tensor
            def _(t, rt=rt, pq=pq, pst=pst, tok0=tok0):
                if e_psum.get(pq):
                    t.wait_ge(s_ev.h, e_psum[pq])
                for pcc in range(NPC):
                    ins = t.matmul(pst, wqrb[:, pcc, rt * 128:(rt + 1) * 128],
                                   h2fm[:, pcc, tok0:tok0 + 512],
                                   start=(pcc == 0), stop=(pcc == NPC - 1))
                ins.then_inc(s_pe.h, 1)
            s_pe.n += 1

            @blk.vector
            def _(v, rt=rt, pst=pst, tok0=tok0):
                s_pe.wait(v)
                v.tensor_copy(out=qret[:, rt, tok0:tok0 + 512], in_=pst
                              ).then_inc(s_ev.h, 1)
            s_ev.n += 1
            e_psum[pq] = s_ev.n

        barrier()
        e_cast = {0: 0, 1: 0}
        e_mm = {0: 0, 1: 0}
        e_psum = {0: 0, 1: 0}

    for t in range(NT):
        emit_transpose16(lambda c, t=t: h2fm[:, c, t * 128:(t + 1) * 128],
                         lambda c, t=t: h2tok[:, t, c * 128:(c + 1) * 128],
                         n_chunks=NPC)

    barrier()

    topk_done = {}

    def emit_topk(t):
        for ph in range(16):
            p_half, h = divmod(ph, NH)
            chunk = (p_half * 512 + h * 64) // 128
            off = (h % 2) * 64
            sps = ps_a if ph % 2 == 0 else ps_c
            bankoff = (ph % 4 // 2) * 128

            @blk.tensor
            def _(t_, t=t, ph=ph, p_half=p_half, h=h, chunk=chunk, off=off,
                  bankoff=bankoff):
                s_v.wait(t_, s_v.n)
                t_.matmul(sps[:, bankoff:bankoff + 128],
                          qret[off:off + 64, chunk, t * 128:(t + 1) * 128],
                          keysT[off:off + 64, p_half, h, :],
                          start=True, stop=True).then_inc(s_pe.h, 1)
            s_pe.n += 1

            @blk.vector
            def _(v, p_half=p_half, h=h, sps=sps, bankoff=bankoff):
                s_pe.wait(v)
                v.max(out=mx01[:, p_half, h * KP:(h + 1) * KP],
                      in_=sps[:, bankoff:bankoff + 128]).then_inc(ch_v.h, 1)
                ch_v.n += 1
                ch_v.wait(v)
                v.max_index(out=mx01[:, p_half, h * KP:(h + 1) * KP][:, 0:KP] if False else ix01[:, p_half, h * KP:(h + 1) * KP],
                            in_max=mx01[:, p_half, h * KP:(h + 1) * KP],
                            in_values=sps[:, bankoff:bankoff + 128]
                            ).then_inc(s_v.h, 1)
            s_v.n += 1

        @blk.vector
        def _(v, t=t):
            v.tensor_tensor(
                out=allsc[:].rearrange("p h (i j) -> p h i j", i=KP),
                in0=mx01[:, 0, :].rearrange("p (h i) -> p h i", i=KP)
                    [:, :, :, None].to_broadcast([128, NH, KP, KP]),
                in1=mx01[:, 1, :].rearrange("p (h j) -> p h j", j=KP)
                    [:, :, None, :].to_broadcast([128, NH, KP, KP]),
                op=AluOpType.add).then_inc(ch_v.h, 1)
            ch_v.n += 1
            ch_v.wait(v)
            for h in range(NH):
                v.max(out=sc8[:, h, :], in_=allsc[:, h, :]).then_inc(ch_v.h, 1)
                ch_v.n += 1
                ch_v.wait(v)
                v.max_index(out=pk8[:, h, :], in_max=sc8[:, h, :],
                            in_values=allsc[:, h, :]).then_inc(ch_v.h, 1)
                ch_v.n += 1
            ch_v.wait(v)
            v.engine_nop()
            v.tensor_scalar(out=pkd[:].rearrange("p h k -> p (h k)"),
                            in0=pk8[:].rearrange("p h k -> p (h k)"),
                            scalar1=3, scalar2=None,
                            op0=AluOpType.logical_shift_right).then_inc(ch_v.h, 1)
            ch_v.n += 1
            v.tensor_scalar(out=pkm[:].rearrange("p h k -> p (h k)"),
                            in0=pk8[:].rearrange("p h k -> p (h k)"),
                            scalar1=7, scalar2=None, op0=AluOpType.bitwise_and
                            ).then_inc(ch_v.h, 1)
            ch_v.n += 1
            ch_v.wait(v)
            v.tensor_tensor(
                out=tsel[:],
                in0=pkd[:][:, :, :, None].to_broadcast([128, NH, KP, KP]),
                in1=iotaj[:][:, None, None, :].to_broadcast([128, NH, KP, KP]),
                op=AluOpType.is_equal).then_inc(ch_v.h, 1)
            ch_v.n += 1
            ch_v.wait(v)
            v.tensor_tensor(
                out=tsel[:], in0=tsel[:],
                in1=ix01[:, 0, :].rearrange("p (h j) -> p h j", j=KP)
                    [:, :, None, :].to_broadcast([128, NH, KP, KP]),
                op=AluOpType.mult).then_inc(ch_v.h, 1)
            ch_v.n += 1
            ch_v.wait(v)
            v.tensor_reduce(out=exK[:], in_=tsel[:], axis=AX.X, op=AluOpType.add
                            ).then_inc(ch_v.h, 1)
            ch_v.n += 1
            ch_v.wait(v)
            v.tensor_tensor(
                out=tsel[:],
                in0=pkm[:][:, :, :, None].to_broadcast([128, NH, KP, KP]),
                in1=iotaj[:][:, None, None, :].to_broadcast([128, NH, KP, KP]),
                op=AluOpType.is_equal).then_inc(ch_v.h, 1)
            ch_v.n += 1
            ch_v.wait(v)
            v.tensor_tensor(
                out=tsel[:], in0=tsel[:],
                in1=ix01[:, 1, :].rearrange("p (h j) -> p h j", j=KP)
                    [:, :, None, :].to_broadcast([128, NH, KP, KP]),
                op=AluOpType.mult).then_inc(ch_v.h, 1)
            ch_v.n += 1
            ch_v.wait(v)
            v.tensor_reduce(out=eyK[:], in_=tsel[:], axis=AX.X, op=AluOpType.add
                            ).then_inc(ch_v.h, 1)
            ch_v.n += 1
            ch_v.wait(v)
            v.tensor_scalar(out=exK[:], in0=exK[:], scalar1=128.0, scalar2=None,
                            op0=AluOpType.mult).then_inc(ch_v.h, 1)
            ch_v.n += 1
            ch_v.wait(v)
            v.tensor_tensor(out=E_all[:, t, :].rearrange("p (h k) -> p h k", k=KP),
                            in0=exK[:], in1=eyK[:], op=AluOpType.add
                            ).then_inc(ch_v.h, 1)
            ch_v.n += 1
            v.tensor_scalar(out=negg[:], in0=sc8[:, :, 0], scalar1=-1.0,
                            scalar2=None, op0=AluOpType.mult).then_inc(s_v.h, 1)
        s_v.n += 1

        @blk.scalar
        def _(sc, t=t):
            s_v.wait(sc)
            for h in range(NH):
                sc.activation(out=gex[:, h, :], in_=sc8[:, h, :], func=AF.Exp,
                              bias=negg[:, h:h + 1]).then_inc(s_sc.h, 1)
        s_sc.n += NH

        @blk.vector
        def _(v, t=t):
            s_sc.wait(v)
            v.tensor_reduce(out=gsum[:], in_=gex[:], axis=AX.X, op=AluOpType.add
                            ).then_inc(ch_v.h, 1)
            ch_v.n += 1
            ch_v.wait(v)
            v.reciprocal(out=gsum[:], in_=gsum[:]).then_inc(ch_v.h, 1)
            ch_v.n += 1
            ch_v.wait(v)
            v.tensor_tensor(out=G_all[:, t, :].rearrange("p (h k) -> p h k", k=KP),
                            in0=gex[:],
                            in1=gsum[:][:, :, None].to_broadcast([128, NH, KP]),
                            op=AluOpType.mult).then_inc(s_v.h, 1)
        s_v.n += 1


        topk_done[t] = s_v.n

    barrier()
    esE.close()


    esG = ExitStack()
    comb_g = sb(esG, "comb_g", [128, 16, DP + D], BF16)
    prodd = sb(esG, "prodd", [128, 4, DP], F32)
    zdot = sb(esG, "zdot", [128, NSLOT], F32)
    zsil = sb(esG, "zsil", [128, NSLOT], F32)
    Dt = sb(esG, "Dt", [128, 8, 128], BF16)
    rload = sb(esG, "rload", [128, 2, D], F32)
    outb = sb(esG, "outb", [128, 2, D], F32)

    @blk.gpsimd
    def _(g):
        g.wait_ge(s_sgout[0].h, s_sgout[0].n)
        g.wait_ge(s_sgout[1].h, s_sgout[1].n)

    mm_after = {}
    dbuild_after = {}
    fin_after = {}

    emit_topk(0)
    for t in range(NT):
        @blk.sync
        def _(sy, t=t):
            sy.dma_start(out=rload[:, t % 2, :],
                         in_=dpark[t * 128:(t + 1) * 128, :]
                         ).then_inc(s_r[t % 2].h, 16)
        s_r[t % 2].n += 16

        for g8 in range(NSLOT // 8):
            s0 = g8 * 8
            if g8 == 2 and t + 1 < NT:
                emit_topk(t + 1)

            @blk.gpsimd
            def _(g, t=t, s0=s0):
                g.wait_ge(s_v.h, topk_done[t])
                for s in range(s0, s0 + 8):
                    gslot = t * NSLOT + s
                    if gslot >= 16:
                        s_vde.wait(g, gslot - 15)
                        g.wait_ge(s_pe.h, mm_after[gslot - 16])
                    g.indirect_dma_start(
                        out=comb_g[:, s % 16, :], out_offset=None, in_=d_tab_bf[:],
                        in_offset=IOffs(E_all[:, t, s:s + 1], 0),
                    ).then_inc(s_de[s % 16].h, 16)
                    s_de[s % 16].n += 16

            @blk.vector
            def _(v, t=t, s0=s0):
                for s in range(s0, s0 + 8):
                    gslot = t * NSLOT + s
                    s_de[s % 16].wait(v)
                    if gslot >= 4:
                        v.wait_ge(s_vde.h, gslot - 3)
                    v.scalar_tensor_tensor(out=prodd[:, s % 4, :],
                                           in0=comb_g[:, s % 16, 0:DP],
                                           scalar=1.0, in1=h2tok[:, t, :],
                                           op0=AluOpType.mult, op1=AluOpType.mult,
                                           accum_out=zdot[:, s:s + 1]
                                           ).then_inc(s_vde.h, 1)
                    s_vde.n += 1
                v.wait_ge(s_vde.h, s_vde.n)
                v.tensor_tensor(out=zsil[:, s0:s0 + 8],
                                in0=zdot[:, s0:s0 + 8],
                                in1=G_all[:, t, s0:s0 + 8],
                                op=AluOpType.mult).then_inc(s_v.h, 1)
            s_v.n += 1

            @blk.scalar
            def _(sc, t=t, s0=s0):
                s_v.wait(sc)
                sc.activation(out=zsil[:, s0:s0 + 8], in_=zsil[:, s0:s0 + 8],
                              func=AF.Silu).then_inc(s_sc.h, 1)
                sc.wait_ge(s_sc.h, s_sc.n + 1)
                for s in range(s0, s0 + 8):
                    gsl = t * NSLOT + s
                    if gsl >= 8:
                        sc.wait_ge(s_pe.h, mm_after[gsl - 8])
                    sc.activation(out=Dt[:, s % 8, :], in_=ident_f[:],
                                  func=AF.Identity, scale=zsil[:, s:s + 1]
                                  ).then_inc(s_sc.h, 1)
                    dbuild_after[gsl] = s_sc.n + 2 + (s - s0)
            s_sc.n += 9

            @blk.tensor
            def _(t_, t=t, s0=s0):
                for s in range(s0, s0 + 8):
                    gsl = t * NSLOT + s
                    t_.wait_ge(s_sc.h, dbuild_after[gsl])
                    t_.wait_ge(s_de[s % 16].h, s_de[s % 16].n)
                    if s == 0 and t > 0:
                        t_.wait_ge(s_v.h, fin_after[t - 1])
                    for k in range(4):
                        ins = t_.matmul(
                            ps_big[:, k * 512:(k + 1) * 512],
                            Dt[:, s % 8, :],
                            comb_g[:, s % 16, DP + k * 512:DP + (k + 1) * 512],
                            start=(s == 0), stop=(s == NSLOT - 1))
                    ins.then_inc(s_pe.h, 1)
                    s_pe.n += 1
                    mm_after[gsl] = s_pe.n

        @blk.vector
        def _(v, t=t):
            s_r[t % 2].wait(v)
            v.wait_ge(s_pe.h, mm_after[t * NSLOT + NSLOT - 1])
            if t >= 2:
                v.wait_ge(s_out2[t % 2].h, 16 * (t // 2))
            v.tensor_tensor(out=outb[:, t % 2, :], in0=ps_big[:, 0:2048],
                            in1=rload[:, t % 2, :], op=AluOpType.add
                            ).then_inc(s_v.h, 1)
        s_v.n += 1
        fin_after[t] = s_v.n

        @blk.sync
        def _(sy, t=t):
            s_v.wait(sy)
            sy.dma_start(out=dout[t * 128:(t + 1) * 128, :], in_=outb[:, t % 2, :]
                         ).then_inc(s_out2[t % 2].h, 16)
        s_out2[t % 2].n += 16

    @blk.sync
    def _(sy):
        sy.wait_ge(s_out2[0].h, s_out2[0].n)
        sy.wait_ge(s_out2[1].h, s_out2[1].n)

    esG.close()
    esH.close()
    es0.close()
    return nc


def kernel(L0_attn_x, L0_attn_w_q, L0_attn_w_kv, L0_attn_w_dense,
           L0_ffn_W_up, L0_ffn_b_up, L0_ffn_W_down, L0_ffn_b_down,
           L0_ffn_W_q, L0_ffn_keys, L0_ffn_down_embed, L0_ffn_up_embed):
    x = np.ascontiguousarray(np.asarray(L0_attn_x, dtype=np.float32))
    weights = {
        "w_q": np.ascontiguousarray(np.asarray(L0_attn_w_q, np.float32).reshape(16, 128, 16, 128).transpose(2, 1, 0, 3)),
        "w_kv": np.ascontiguousarray(np.asarray(L0_attn_w_kv, np.float32).reshape(16, 128, 8, 128).transpose(2, 1, 0, 3)),
        "w_dense": np.ascontiguousarray(np.asarray(L0_attn_w_dense, np.float32).reshape(16, 128, 16, 128).transpose(2, 1, 0, 3)),
        "w_up": np.ascontiguousarray(np.asarray(L0_ffn_W_up, np.float32).T.reshape(16, 128, 64, 128).transpose(2, 1, 0, 3)),
        "b_up": np.ascontiguousarray(np.asarray(L0_ffn_b_up, np.float32)),
        "w_down": np.ascontiguousarray(np.asarray(L0_ffn_W_down, np.float32).T.reshape(8, 8, 128, 8, 128).transpose(3, 0, 2, 1, 4)),
        "b_down": np.ascontiguousarray(np.asarray(L0_ffn_b_down, np.float32)),
        "w_qr": np.ascontiguousarray(np.asarray(L0_ffn_W_q, np.float32).T),
        "keys": np.ascontiguousarray(np.asarray(L0_ffn_keys, np.float32)),
        "tab": np.ascontiguousarray(np.concatenate(
            [np.asarray(L0_ffn_down_embed, np.float32),
             np.asarray(L0_ffn_up_embed, np.float32)], axis=1)),
    }

    nc = build_nc()
    in_maps = []
    tiles_per_core = []
    for c in range(8):
        b = c // 2
        tiles = TILES_EVEN if c % 2 == 0 else TILES_ODD
        tiles_per_core.append(tiles)
        rows = np.concatenate([np.arange(g * 128, (g + 1) * 128) for g in tiles])
        qidx = np.empty((128, NT), np.float32)
        for i, g in enumerate(tiles):
            qidx[:, i] = g * 128 + np.arange(128)
        m = {"x_own": np.ascontiguousarray(x[b][rows]), "x_seq": x[b],
             "qidx": qidx}
        m.update(weights)
        in_maps.append(m)

    res = run_bass_kernel_spmd(nc, in_maps, core_ids=list(range(8)))

    out = np.empty((B, SEQ, D), np.float32)
    for c in range(8):
        tiles = tiles_per_core[c]
        o = res.results[c]["out"]
        b = c // 2
        for i, g in enumerate(tiles):
            out[b, g * 128:(g + 1) * 128, :] = o[i * 128:(i + 1) * 128, :]
    return out
